# version 28
# baseline (speedup 1.0000x reference)
import contextlib
import numpy as np
import concourse.bass as bass
import concourse.mybir as mybir
from concourse.bass_utils import run_bass_kernel_spmd

F32 = mybir.dt.float32
BF16 = mybir.dt.bfloat16
FP8 = mybir.dt.float8e4
AF = mybir.ActivationFunctionType
ALU = mybir.AluOpType
AX = mybir.AxisListType
ds = bass.ds

S_SEQ, D, H4 = 8192, 1024, 4096
OWN = 4096
NTILE, TTOK, NSEG = 16, 256, 2
CC_D, CC_H = 8, 32
LN_EPS = 1e-5
ATTN_EPS = 1e-6
CMAG = 1.5 * 2.0 ** 23
CMAG52 = 1.5 * 2.0 ** 52
S29 = 2.0 ** -29
P29 = 2.0 ** 29
JN = 2056
IN_ = 1032
AFULL = 2056
H0FULL = 4104
L1PAD = 4


class Prog:
    def __init__(self, nc):
        self.nc = nc
        self.streams = {e: [] for e in ("sync", "vector", "scalar", "tensor", "gpsimd")}
        self.cnt = {}
        self.sems = {}

    def emit(self, eng, fn):
        self.streams[eng].append(fn)

    def wait(self, eng, sem, val=None):
        v_ = self.cnt.get(sem, 0) if val is None else val
        if v_ <= 0:
            return
        s_, h = self.sems[sem], getattr(self.nc, eng)
        self.emit(eng, lambda: h.wait_ge(s_, v_))

    def mark(self, sem):
        return self.cnt.get(sem, 0)

    def run(self, eng, fn, inc=None, n=1):
        if inc is None:
            self.emit(eng, fn)
        else:
            s_ = self.sems[inc]
            self.cnt[inc] = self.cnt.get(inc, 0) + n
            self.emit(eng, lambda: fn().then_inc(s_, n))

    def join(self, skip=()):
        for eng in ("sync", "vector", "scalar", "tensor", "gpsimd"):
            for sem in ("sLd", "sTr", "sGp", "sSt", "sV", "sS", "sE", "sG", "sVr"):
                if sem not in skip:
                    self.wait(eng, sem)


ENG_SEM = {"vector": "sV", "scalar": "sS", "tensor": "sE", "gpsimd": "sG", "sync": "sLd"}


class Banks:

    def __init__(self, p):
        self.p = p
        self.last = {}

    def use(self, bank, producer_eng="tensor"):
        ev = self.last.get(bank)
        if ev is not None:
            sem, cnt = ev
            self.p.wait(producer_eng, sem, cnt)

    def consumed(self, bank, eng):
        sem = ENG_SEM[eng]
        self.last[bank] = (sem, self.p.mark(sem))


def build_program(dbg=()):
    nc = bass.Bass("TRN2", target_bir_lowering=False)
    dt = {}

    def din(name, shape, dtype=F32):
        dt[name] = nc.dram_tensor(name, shape, dtype, kind="ExternalInput").ap()

    def dout(name, shape, dtype=F32):
        dt[name] = nc.dram_tensor(name, shape, dtype, kind="ExternalOutput").ap()

    din("x", [S_SEQ, D])
    din("xo", [OWN, D])
    for n_ in ("hfT", "gfT", "hsT", "gsT"):
        din(n_, [D, 4])
    for n_ in ("gna", "bna", "g1", "c1", "b2"):
        din(n_, [D])
    for n_ in ("g2", "c2", "b1"):
        din(n_, [H4])
    din("w1t", [D, H4])
    din("w2t", [H4, D])
    dout("y", [OWN, D])
    dt["h0hbm"] = nc.dram_tensor("h0hbm", [128, CC_D, H0FULL], BF16, kind="Internal").ap()
    dt["lrhbm"] = nc.dram_tensor("lrhbm", [128, CC_D, H0FULL], BF16, kind="Internal").ap()

    dbg_shapes = {
        "low2": ([128, CC_D, 2048], BF16),
        "h0b": ([128, CC_D, H0FULL], BF16),
        "h1b": ([128, CC_D, AFULL], BF16),
        "qb": ([128, CC_D, 2048], BF16),
        "kvb": ([128, CC_D, 128], BF16),
        "zbd": ([128, CC_D, 16], BF16),
        "attn": ([128, CC_D, AFULL], BF16),
        "lrb": ([128, CC_D, H0FULL], BF16),
        "m1c": ([128, CC_D, H4], BF16),
        "m2c": ([128, CC_H, D], BF16),
        "wsc": ([128, 8], F32),
        "lowt": ([NTILE, 128, NSEG, D], BF16),
        "rt": ([NTILE, 128, NSEG, D], BF16),
        "wt": ([NTILE, 128, NSEG, D], BF16),
        "x1t": ([NTILE, 128, NSEG, D], BF16),
        "l1t": ([NTILE, 128, NSEG, D], BF16),
        "n1t": ([NTILE, 128, NSEG, D], BF16),
        "hpre": ([NTILE, NSEG, 128, H4], BF16),
        "gt": ([NTILE, NSEG, 128, H4], BF16),
        "l2t": ([NTILE, NSEG, 128, H4], BF16),
        "n2t": ([NTILE, NSEG, 128, H4], BF16),
        "st2": ([NTILE, 128, 64], F32),
    }
    for name in dbg:
        dout("dbg_" + name, *dbg_shapes[name])

    with contextlib.ExitStack() as st:
        p = Prog(nc)
        _build(nc, p, st, dt, set(dbg))
    return nc


def _build(nc, p, st, dt, dbg):
    def sb(stk, name, shape, dtype, side="left"):
        return stk.enter_context(nc.sbuf_tensor(name, shape, dtype, side=side))
    V, S, T, G, SY = nc.vector, nc.scalar, nc.tensor, nc.gpsimd, nc.sync

    fvec = sb(st, "fvec", [128, 4, CC_D, 4], F32)
    frec = sb(st, "frec", [128, 10, CC_D], F32)
    dmask = sb(st, "dmask", [128, 128], BF16)
    consts = sb(st, "consts", [128, 8], F32)
    selt = sb(st, "selt", [16, CC_D, 128], F32)
    zbd = sb(st, "zbd", [128, CC_D, 16], BF16)
    stat = sb(st, "stat", [128, 64], F32)
    stath = sb(st, "stath", [128, 64], F32)
    onesr = sb(st, "onesr", [1, 128], BF16)
    onec = sb(st, "onec", [128, 4], BF16)
    onesf = sb(st, "onesf", [128, 4], F32)
    onesfr = sb(st, "onesfr", [1, 128], F32)
    z8 = sb(st, "z8", [128, 8, 4], BF16)
    stI = contextlib.ExitStack()
    iosc = sb(stI, "iosc", [128, 8, 128], F32)

    blk = st.enter_context(nc.Block())
    for s_ in ("sLd", "sTr", "sGp", "sSt", "sV", "sS", "sE", "sG", "sVr"):
        p.sems[s_] = st.enter_context(nc.semaphore(s_))

    banks = Banks(p)
    marks = {}

    def setmark(name, sem):
        marks[name] = (sem, p.mark(sem))

    def waitm(eng, name):
        if name in marks:
            sem, val = marks[name]
            p.wait(eng, sem, val)
    _hfreg = {}

    def hf_val(eng_name):
        if eng_name not in _hfreg:
            eng = getattr(nc, eng_name)
            _hfreg[eng_name] = eng.compute_val(eng.partition_id() % 2)
        return _hfreg[eng_name]
    ldma = lambda fn: p.run("sync", fn, "sLd", 16)
    sdma = lambda fn: p.run("sync", fn, "sSt", 16)
    gdma = lambda fn: p.run("gpsimd", fn, "sGp", 16)
    trdma = lambda fn: p.run("sync", fn, "sTr", 16)

    def v(fn, ch=False):
        if ch:
            p.wait("vector", "sV")
        p.run("vector", fn, "sV")

    def vc(fn):
        v(fn, ch=True)

    def s(fn, ch=False):
        if ch:
            p.wait("scalar", "sS")
        p.run("scalar", fn, "sS")

    def t(fn, ch=False):
        if ch:
            p.wait("tensor", "sE")
        p.run("tensor", fn, "sE")

    def g(fn):
        p.wait("gpsimd", "sG")
        p.run("gpsimd", fn, "sG")

    def tr_fence(full=True):
        if full:
            for sem in ("sLd", "sGp", "sSt", "sHL", "sX0", "sX1"):
                if sem in p.sems:
                    p.wait("sync", sem)
        p.wait("sync", "sTr")

    def dbg_dump(name, src_ap):
        if name not in dbg:
            return
        for sem in ("sV", "sS", "sE", "sGp"):
            p.wait("sync", sem)
        sdma(lambda: SY.dma_start(out=dt["dbg_" + name][:], in_=src_ap))

    pcv = st.enter_context(nc.psum_tensor("pcv", [128, 4, 512], F32))
    pkv = st.enter_context(nc.psum_tensor("pkv", [128, 2, 512], F32))
    pbc = st.enter_context(nc.psum_tensor("pbc", [128, 2, 512], F32))

    def conv_block(ps_sl, taps, bank):
        banks.use(bank)
        for i, (dg, rhs) in enumerate(taps):
            t(lambda dg=dg, rhs=rhs, i=i, last=(i == len(taps) - 1):
              T.matmul(ps_sl, dg, rhs, start=(i == 0), stop=last))

    ldma(lambda: SY.dma_start(out=fvec[:, 0], in_=dt["hfT"].rearrange("(cc p) k -> p cc k", p=128)))
    ldma(lambda: SY.dma_start(out=fvec[:, 1], in_=dt["gfT"].rearrange("(cc p) k -> p cc k", p=128)))
    ldma(lambda: SY.dma_start(out=fvec[:, 2], in_=dt["hsT"].rearrange("(cc p) k -> p cc k", p=128)))
    ldma(lambda: SY.dma_start(out=fvec[:, 3], in_=dt["gsT"].rearrange("(cc p) k -> p cc k", p=128)))
    p.wait("vector", "sLd")

    for fi, base in ((2, 0), (3, 5)):
        vc(lambda fi=fi, base=base: V.tensor_copy(frec[:, base], fvec[:, fi, :, 0]))
        vc(lambda fi=fi, base=base: V.tensor_add(frec[:, base + 1], fvec[:, fi, :, 1], fvec[:, fi, :, 2]))
        vc(lambda fi=fi, base=base: V.tensor_copy(frec[:, base + 2], fvec[:, fi, :, 3]))
        vc(lambda fi=fi, base=base: V.tensor_add(frec[:, base + 3], fvec[:, fi, :, 0], fvec[:, fi, :, 1]))
        vc(lambda fi=fi, base=base: V.tensor_add(frec[:, base + 4], fvec[:, fi, :, 2], fvec[:, fi, :, 3]))

    p.run("gpsimd", lambda: G.iota(iosc[:, 0], [[1, 128]], base=0, channel_multiplier=-1, allow_small_or_imprecise_dtypes=True), "sG")
    p.run("gpsimd", lambda: G.iota(iosc[:16, 1], [[1, 128]], base=0, channel_multiplier=0, allow_small_or_imprecise_dtypes=True), "sG")
    p.run("gpsimd", lambda: G.iota(iosc[:16, 2], [[0, 128]], base=0, channel_multiplier=1, allow_small_or_imprecise_dtypes=True), "sG")
    p.wait("vector", "sG")
    vc(lambda: V.tensor_scalar(dmask[:], iosc[:, 0], 0.0, None, ALU.is_equal))
    vc(lambda: V.tensor_scalar(iosc[:16, 3], iosc[:16, 1], 63.0, None, ALU.is_gt))
    for cc in range(CC_D):
        vc(lambda cc=cc: V.tensor_scalar(iosc[:16, 4], iosc[:16, 3], float(2 * cc), None, ALU.add))
        vc(lambda cc=cc: V.tensor_tensor(selt[:, cc], iosc[:16, 4], iosc[:16, 2], ALU.is_equal))

    import os
    KCUT = int(os.environ.get("KDBG_CUT", "0"))
    class _CutDone(Exception):
        pass
    def cut(n):
        if KCUT == n:
            for sem in ("sV", "sS", "sE", "sG", "sGp", "sTr", "sLd"):
                p.wait("sync", sem)
            sdma(lambda: SY.dma_start(out=dt["y"][0:16, :128], in_=selt[:, 7]))
            p.wait("sync", "sSt")
            raise _CutDone()
    try:
        cut(1)
    except _CutDone:
        _emit_streams(blk, p)
        return
    OPEN_STACKS = []
    vc(lambda: V.memset(consts[:, 0:1], S29))
    vc(lambda: V.tensor_scalar(consts[:, 1:2], consts[:, 0:1], CMAG, -CMAG, ALU.add, ALU.add))
    vc(lambda: V.tensor_scalar(consts[:, 2:3], consts[:, 1:2], P29, None, ALU.mult))
    vc(lambda: V.tensor_scalar(consts[:, 3:4], consts[:, 2:3], -1.0, 1.0, ALU.mult, ALU.add))
    vc(lambda: V.memset(stat[:], 0.0))
    vc(lambda: V.memset(stath[:], 0.0))
    v(lambda: V.memset(onesr[:], 1.0))
    v(lambda: V.memset(onec[:], 1.0))
    v(lambda: V.memset(onesf[:], 1.0))
    v(lambda: V.memset(onesfr[:], 1.0))

    stI.close()
    stP1 = contextlib.ExitStack(); OPEN_STACKS.append(stP1)
    low2 = sb(stP1, "low2", [128, CC_D, 2112], BF16)
    h1b = sb(stP1, "h1b", [128, CC_D, 2112], BF16)

    stA = contextlib.ExitStack(); OPEN_STACKS.append(stA)
    xst = sb(stA, "xst", [128, CC_D, 10, 128], BF16)
    xf = sb(stA, "xf", [128, 2, 80, 128], BF16)
    low1w = sb(stA, "low1w", [128, CC_D, 1056], BF16)
    spst = sb(stA, "spst", [128, CC_D, 520], BF16)
    diagf = sb(stA, "diagf", [128, 8, CC_D, 128], BF16)

    v(lambda: V.memset(z8[:], 0.0))
    v(lambda: V.memset(xst[:, :, 0, :], 0.0))
    v(lambda: V.memset(spst[:], 0.0))
    setmark("xst0z", "sV")
    for fi in range(2):
        for k in range(4):
            for cc in range(CC_D):
                v(lambda fi=fi, k=k, cc=cc: V.tensor_scalar(
                    diagf[:, fi * 4 + k, cc], dmask[:], fvec[:, fi, cc, k:k + 1], None, ALU.mult))
    v(lambda: V.memset(low1w[:], 0.0))
    v(lambda: V.memset(low2[:], 0.0))
    v(lambda: V.memset(h1b[:], 0.0))

    try:
        cut(2)
    except _CutDone:
        _emit_streams(blk, p)
        for _s in reversed(OPEN_STACKS):
            _s.close()
        return
    BLK1 = L1PAD + 512
    for k in range(8):
        buf = k % 2
        lo = 1024 * k - 128
        p.wait("gpsimd", "sTr")
        p.wait("vector", "sTr")
        for r in range(10):
            tk = lo + r * 128
            if tk < 0 or tk >= S_SEQ:
                if k > 0:
                    v(lambda r=r: V.memset(xst[:, :, r, :], 0.0))
            else:
                gdma(lambda tk=tk, r=r: G.dma_start(
                    out=xst[:, :, r, :],
                    in_=dt["x"][tk:tk + 128, :].rearrange("p (cc c) -> p cc c", cc=CC_D)))
        p.wait("sync", "sGp")
        if k == 0:
            waitm("sync", "xst0z")
        elif k == 7:
            p.wait("sync", "sV")
        waitm("sync", "xf_used_%d" % buf)
        tr_fence()
        trdma(lambda buf=buf: SY.dma_start_transpose(
            xf[:, buf], xst.rearrange("p a r c -> p (a r c)")))
        p.wait("tensor", "sTr")
        xfb = xf[:, buf].rearrange("p a b -> p (a b)")
        if k == 0:
            p.wait("tensor", "sV")

        for cc in range(CC_D):
            bank = cc % 2
            taps = []
            for d_ in range(4):
                f0 = 128 + (d_ - 1)
                taps.append((diagf[:, d_, cc], xfb[:, cc * 1280 + f0: cc * 1280 + f0 + 1024:2]))
            conv_block(pcv[:, bank, :], taps, bank)
            p.wait("scalar", "sE")
            s(lambda cc=cc: S.copy(low1w[:, cc, BLK1:BLK1 + 512], pcv[:, cc % 2, :]))
            banks.consumed(bank, "scalar")
        for cc in range(CC_D):
            bank = 2 + cc % 2
            taps = []
            for d_ in range(4):
                f0 = 128 + (d_ - 1)
                taps.append((diagf[:, 4 + d_, cc], xfb[:, cc * 1280 + f0: cc * 1280 + f0 + 1024:2]))
            conv_block(pcv[:, bank, :], taps, bank)
            p.wait("vector", "sE")
            p.wait("vector", "sSt")
            v(lambda cc=cc: V.tensor_copy(spst[:, cc, 4:516], pcv[:, 2 + cc % 2, :512]))
            banks.consumed(bank, "vector")
        setmark("xf_used_%d" % buf, "sE")
        p.wait("sync", "sV")
        if k == 0:
            sdma(lambda: SY.dma_start(out=dt["h0hbm"][:, :, 0:516], in_=spst[:, :, 0:516]))
        elif k == 7:
            sdma(lambda: SY.dma_start(out=dt["h0hbm"][:, :, 4 + 512 * 7:H0FULL],
                                      in_=spst[:, :, 4:520]))
        else:
            sdma(lambda k=k: SY.dma_start(out=dt["h0hbm"][:, :, 4 + 512 * k:4 + 512 * k + 512],
                                          in_=spst[:, :, 4:516]))

        i0 = -2 if k == 0 else 256 * k - 2
        i1 = 2048 if k == 7 else 256 * (k + 1) - 2
        base_j = 512 * (k - 1)
        p.wait("tensor", "sS")
        for q0 in range(i0, i1, 512):
            qn = min(512, i1 - q0)
            for cc in range(CC_D):
                bank = cc % 2
                taps = []
                for d_ in range(4):
                    colf = 2 * q0 + d_ - 1 - base_j + L1PAD
                    taps.append((diagf[:, d_, cc], low1w[:, cc, colf:colf + 2 * qn:2]))
                conv_block(pcv[:, bank, :qn], taps, bank)
                p.wait("scalar", "sE")
                c0 = max(q0, 0)
                if c0 < q0 + qn:
                    s(lambda cc=cc, c0=c0, q0=q0, qn=qn: S.copy(
                        low2[:, cc, c0:q0 + qn], pcv[:, cc % 2, c0 - q0:qn]))
                banks.consumed(bank, "scalar")
            h0_ = max(q0, 0)
            hskip = h0_ - q0
            if h0_ < q0 + qn:
                for cc in range(CC_D):
                    bank = 2 + cc % 2
                    taps = []
                    for d_ in range(4):
                        colf = 2 * h0_ + d_ - 1 - base_j + L1PAD
                        taps.append((diagf[:, 4 + d_, cc], low1w[:, cc, colf:colf + 2 * (qn - hskip):2]))
                    conv_block(pcv[:, bank, :qn - hskip], taps, bank)
                    p.wait("vector", "sE")
                    v(lambda cc=cc, h0_=h0_, q0=q0, qn=qn, hskip=hskip: V.tensor_copy(
                        h1b[:, cc, h0_ + 4:q0 + 4 + qn], pcv[:, 2 + cc % 2, :qn - hskip]))
                    banks.consumed(bank, "vector")
        p.wait("vector", "sE")
        p.wait("vector", "sS")
        for cc in range(CC_D):
            v(lambda cc=cc: V.tensor_copy(low1w[:, cc, 0:516], low1w[:, cc, 512:1028]))

    try:
        cut(3)
    except _CutDone:
        _emit_streams(blk, p)
        for _s in reversed(OPEN_STACKS):
            _s.close()
        return
    dbg_dump("low2", low2[:, :, :2048])
    dbg_dump("h1b", h1b[:, :, :AFULL])
    p.join()
    stA.close(); OPEN_STACKS.remove(stA)

    attnb = sb(stP1, "attnb", [128, CC_D, 2112], BF16)
    stB = contextlib.ExitStack(); OPEN_STACKS.append(stB)
    qb = sb(stB, "qb", [128, CC_D, 2112], BF16)
    qTs = sb(stB, "qTs", [128, 2, 16, 128], BF16)
    vTs = sb(stB, "vTs", [128, 2, 16, 128], BF16)
    kvt = sb(stB, "kvt", [128, CC_D, 128], BF16)

    v(lambda: V.memset(attnb[:], 0.0))
    for cc in range(CC_D):
        v(lambda cc=cc: V.tensor_scalar(qb[:, cc, :2048], low2[:, cc, :2048], 0.0, None, ALU.min))
    p.wait("scalar", "sV")
    for cc in range(CC_D):
        s(lambda cc=cc: S.activation(qb[:, cc, :2048], qb[:, cc, :2048], AF.Exp))
    p.wait("vector", "sS")
    for cc in range(CC_D):
        v(lambda cc=cc: V.scalar_tensor_tensor(qb[:, cc, :2048], low2[:, cc, :2048], 0.0,
                                               qb[:, cc, :2048], ALU.max, ALU.add))
    dbg_dump("qb", qb[:, :, :2048])

    p.wait("sync", "sV")
    p.wait("sync", "sS")
    v(lambda: V.memset(kvt[:], 0.0))
    vc(lambda: V.memset(zbd[:], 0.0))
    p.wait("tensor", "sV")
    for cc in range(CC_D):
        b2_ = cc % 2
        p.wait("sync", "sE")
        tr_fence()
        trdma(lambda cc=cc, b2_=b2_: SY.dma_start_transpose(qTs[:, b2_], qb[:, cc, :2048]))
        tr_fence()
        trdma(lambda cc=cc, b2_=b2_: SY.dma_start_transpose(vTs[:, b2_], low2[:, cc, :2048]))
        p.wait("tensor", "sTr")
        bank = 4 + cc % 2
        banks.use(bank)
        for jn in range(16):
            t(lambda cc=cc, jn=jn, b2_=b2_: T.matmul(pkv[:, cc % 2, 0:128], qTs[:, b2_, jn],
                                                     vTs[:, b2_, jn],
                                                     start=(jn == 0), stop=(jn == 15)))
        for jn in range(16):
            t(lambda cc=cc, jn=jn, b2_=b2_: T.matmul(pkv[:, cc % 2, 128:129], qTs[:, b2_, jn],
                                                     onec[:, 0:1],
                                                     start=(jn == 0), stop=(jn == 15)))
        p.wait("vector", "sE")
        v(lambda cc=cc: V.tensor_copy(kvt[0:64, cc, 0:64], pkv[0:64, cc % 2, 0:64]))
        v(lambda cc=cc: V.tensor_copy(kvt[64:128, cc, 64:128], pkv[64:128, cc % 2, 64:128]))
        vc(lambda cc=cc: V.tensor_copy(zbd[0:64, cc, 2 * cc:2 * cc + 1], pkv[0:64, cc % 2, 128:129]))
        vc(lambda cc=cc: V.tensor_copy(zbd[64:128, cc, 2 * cc + 1:2 * cc + 2],
                                      pkv[64:128, cc % 2, 128:129]))
        banks.consumed(bank, "vector")
    dbg_dump("kvb", kvt[:])
    dbg_dump("zbd", zbd[:])

    rdent = sb(stB, "rdent", [16, 2048], F32)
    rden = rdent[:]
    p.wait("tensor", "sV")
    for q0 in range(0, 2048, 512):
        banks.use(6)
        for cc in range(CC_D):
            t(lambda cc=cc, q0=q0: T.matmul(
                pbc[0:16, 0, :], zbd[:, cc], qb[:, cc, q0:q0 + 512],
                start=(cc == 0), stop=(cc == CC_D - 1)))
        p.wait("vector", "sE")
        v(lambda q0=q0: V.tensor_scalar(rden[:, q0:q0 + 512], pbc[0:16, 0, :],
                                        ATTN_EPS, None, ALU.add))
        v(lambda q0=q0: V.reciprocal(rden[:, q0:q0 + 512], rden[:, q0:q0 + 512]))
        banks.consumed(6, "vector")

    bcs = sb(stB, "bcs", [128, 2, 512], BF16)
    p.wait("tensor", "sV")
    for cc in range(CC_D):
        for q0 in range(0, 2048, 512):
            par = (q0 // 512) % 2
            banks.use(6 + par)
            t(lambda cc=cc, q0=q0, par=par: T.matmul(pbc[:, par, :], selt[:, cc],
                                                     rden[:, q0:q0 + 512], start=True, stop=True))
            p.wait("scalar", "sE")
            p.wait("scalar", "sV", banks.last.get(("bcs", par), (None, 0))[1])
            s(lambda par=par: S.copy(bcs[:, par, :], pbc[:, par, :]))
            banks.consumed(6 + par, "scalar")
            banks.use(4 + par)
            t(lambda cc=cc, q0=q0, par=par: T.matmul(pkv[:, par, :], kvt[:, cc],
                                                     qb[:, cc, q0:q0 + 512], start=True, stop=True))
            p.wait("vector", "sE")
            p.wait("vector", "sS")
            v(lambda cc=cc, q0=q0, par=par: V.tensor_tensor(
                attnb[:, cc, 4 + q0:4 + q0 + 512], pkv[:, par, :], bcs[:, par, :], ALU.mult))
            banks.consumed(4 + par, "vector")
            banks.last[("bcs", par)] = ("sV", p.cnt.get("sV", 0))
    try:
        cut(4)
    except _CutDone:
        _emit_streams(blk, p)
        for _s in reversed(OPEN_STACKS):
            _s.close()
        return
    dbg_dump("attn", attnb[:, :, :AFULL])
    p.join()
    stB.close(); OPEN_STACKS.remove(stB)

    stC = contextlib.ExitStack(); OPEN_STACKS.append(stC)
    att_own = sb(stC, "att_own", [128, CC_D, 1040], BF16)
    h1_own = sb(stC, "h1_own", [128, CC_D, 1040], BF16)
    spst2 = sb(stC, "spst2", [128, CC_D, 1032], BF16)
    dgrec = sb(st, "dgrec", [128, 10, CC_D, 128], BF16, side="right")

    for ti in range(10):
        for cc in range(CC_D):
            v(lambda ti=ti, cc=cc: V.tensor_scalar(
                dgrec[:, ti, cc], dmask[:], frec[:, ti, cc:cc + 1], None, ALU.mult))

    DBG_STATIC_HF = -1
    if "lrb" in dbg:
        vc(lambda: V.memset(spst2[:], 0.0))
        p.wait("sync", "sV")
        for q0 in range(0, H0FULL, 1026):
            qn = min(1026, H0FULL - q0)
            sdma(lambda q0=q0, qn=qn: SY.dma_start(out=dt["lrhbm"][:, :, q0:q0 + qn],
                                                   in_=spst2[:, :, :qn]))
        p.wait("scalar", "sSt")

    def emit_own_copies():
        if DBG_STATIC_HF >= 0:
            off = DBG_STATIC_HF * 1024
            for cc in range(CC_D):
                V.tensor_copy(att_own[:, cc, 0:1036], attnb[:, cc, off:off + 1036])
                V.tensor_copy(h1_own[:, cc, 0:1036], h1b[:, cc, off:off + 1036])
            return
        off = hf_val("vector") * 1024
        for cc in range(CC_D):
            V.tensor_copy(att_own[:, cc, 0:1036], attnb[:, cc, ds(off, 1036)])
            V.tensor_copy(h1_own[:, cc, 0:1036], h1b[:, cc, ds(off, 1036)])
    p.emit("vector", emit_own_copies)
    vc(lambda: V.tensor_copy(stat[:, 0:1], consts[:, 0:1]))

    p.wait("scalar", "sV")
    p.wait("tensor", "sV")
    for jb in range(0, JN, 1024):
        je = min(jb + 1024, JN)
        n_e = (je - jb) // 2
        n_o = (je - jb) - n_e
        ue = 2 + jb // 2
        for cc in range(CC_D):
            bank = cc % 2
            taps = [(dgrec[:, 0, cc], att_own[:, cc, ue - 1:ue - 1 + n_e]),
                    (dgrec[:, 1, cc], att_own[:, cc, ue:ue + n_e]),
                    (dgrec[:, 2, cc], att_own[:, cc, ue + 1:ue + 1 + n_e]),
                    (dgrec[:, 5, cc], h1_own[:, cc, ue - 1:ue - 1 + n_e]),
                    (dgrec[:, 6, cc], h1_own[:, cc, ue:ue + n_e]),
                    (dgrec[:, 7, cc], h1_own[:, cc, ue + 1:ue + 1 + n_e])]
            conv_block(pcv[:, bank, :n_e], taps, bank)
            bank2 = 2 + cc % 2
            taps = [(dgrec[:, 3, cc], att_own[:, cc, ue:ue + n_o]),
                    (dgrec[:, 4, cc], att_own[:, cc, ue + 1:ue + 1 + n_o]),
                    (dgrec[:, 8, cc], h1_own[:, cc, ue:ue + n_o]),
                    (dgrec[:, 9, cc], h1_own[:, cc, ue + 1:ue + 1 + n_o])]
            conv_block(pcv[:, bank2, :n_o], taps, bank2)
            p.wait("scalar", "sE")
            p.wait("scalar", "sSt")
            s(lambda cc=cc, n_e=n_e: S.copy(spst2[:, cc, 0:2 * n_e:2], pcv[:, cc % 2, :n_e]))
            s(lambda cc=cc, n_o=n_o: S.copy(spst2[:, cc, 1:1 + 2 * n_o:2], pcv[:, 2 + cc % 2, :n_o]))
            banks.consumed(cc % 2, "scalar")
            banks.consumed(2 + cc % 2, "scalar")
        p.wait("sync", "sS")
        def spill_lr(jb=jb, je=je):
            off = hf_val("sync") * 2048 + jb
            return SY.dma_start(out=dt["lrhbm"][:, :, ds(off, je - jb)],
                                in_=spst2[:, :, :je - jb])
        sdma(spill_lr)

    p.wait("gpsimd", "sSt")
    gdma(lambda: G.dma_start(out=dt["lrhbm"][:, :, 0:4], in_=z8[:]))
    gdma(lambda: G.dma_start(out=dt["lrhbm"][:, :, H0FULL - 4:H0FULL], in_=z8[:]))
    if "lrb" in dbg:
        p.wait("sync", "sSt")
        ldma(lambda: SY.dma_start(out=dt["dbg_lrb"][:], in_=dt["lrhbm"][:]))
    if "h0b" in dbg:
        p.wait("sync", "sSt")
        ldma(lambda: SY.dma_start(out=dt["dbg_h0b"][:], in_=dt["h0hbm"][:]))
    p.join(skip=("sGp",))
    stC.close(); OPEN_STACKS.remove(stC)
    stP1.close(); OPEN_STACKS.remove(stP1)

    m1 = sb(st, "m1", [128, CC_D, H4], FP8, side="right")
    m2 = sb(st, "m2", [128, CC_H, D], FP8, side="right")
    gbd = sb(st, "gbd", [128, 5, D], BF16, side="right")
    gbh = sb(st, "gbh", [128, 3, H4], BF16, side="right")
    stD = contextlib.ExitStack(); OPEN_STACKS.append(stD)
    wstage = sb(stD, "wstage", [128, 2, 4096], F32)
    stDv = contextlib.ExitStack()
    vrow2 = sb(stDv, "vrow2", [1, 2 * 4224], F32)

    vdma = lambda fn: p.run("sync", fn, "sVr", 16)

    allv = [(gbd, 0, "gna"), (gbd, 1, "bna"), (gbd, 2, "g1"), (gbd, 3, "c1"),
            (gbd, 4, "b2"), (gbh, 0, "g2"), (gbh, 1, "c2"), (gbh, 2, "b1")]
    bqc = 0
    vwm = {}
    for idx, (dst3, slot, name) in enumerate(allv):
        nf = dt[name].shape[0]
        half = (idx % 2) * 4224
        waitm("sync", "vbc%d" % (idx % 2))
        vdma(lambda half=half, name=name, nf=nf: SY.dma_start(
            out=vrow2[:, half:half + nf], in_=dt[name][None, :]))
        p.wait("tensor", "sVr")
        for q in range(nf // 512):
            bank = 6 + bqc % 2
            bqc += 1
            banks.use(bank)
            t(lambda half=half, q=q, bank=bank: T.matmul(
                pbank(bank), onesfr[:], vrow2[:, half + q * 512:half + (q + 1) * 512],
                start=True, stop=True))
            p.wait("scalar", "sE")
            s(lambda q=q, dst3=dst3, slot=slot, bank=bank: S.copy(
                dst3[:, slot, q * 512:(q + 1) * 512], pbank(bank)))
            banks.consumed(bank, "scalar")
        setmark("vbc%d" % (idx % 2), "sE")
    stDv.close()
    wsgn = sb(stD, "wsgn", [128, 4, 4096], BF16)

    def ternarize(w_name, n_cc, free_n, m_tile, wsc_idx):
        for s_ in ("sW0", "sW1"):
            if s_ not in p.sems:
                p.sems[s_] = st.enter_context(nc.semaphore(s_))
        wsem = ["sW0", "sW1"]
        wm = {}
        for ccx in range(n_cc):
            b_ = ccx % 2
            waitm("sync", "wslab%d" % b_)
            waitm("sync", "wslabS%d" % b_)
            p.run("sync", lambda ccx=ccx, b_=b_, free_n=free_n, w_name=w_name: SY.dma_start(
                out=wstage[:, b_, :free_n], in_=dt[w_name][ccx * 128:(ccx + 1) * 128, :]),
                wsem[b_], 16)
            wm[ccx] = (wsem[b_], p.mark(wsem[b_]))
            p.wait("vector", *wm[ccx])
            vc(lambda ccx=ccx, b_=b_, free_n=free_n: V.tensor_reduce(
                stath[:, ccx:ccx + 1], wstage[:, b_, :free_n], AX.X, ALU.add,
                apply_absolute_value=True))
            setmark("wslab%d" % b_, "sV")
        vc(lambda n_cc=n_cc: V.tensor_reduce(stat[:, 42:43], stath[:, 0:n_cc], AX.X, ALU.add))
        p.wait("tensor", "sV")
        banks.use(6)
        t(lambda: T.matmul(pbc[0:1, 0, 0:1], stat[:, 42:43], onesf[:, 0:1],
                           start=True, stop=True))
        p.wait("vector", "sE")
        nel = n_cc * 128 * free_n
        vc(lambda: V.tensor_copy(stat[0:1, 43:44], pbc[0:1, 0, 0:1]))
        banks.consumed(6, "vector")
        vc(lambda nel=nel: V.tensor_scalar(stat[0:1, 44:45], stat[0:1, 43:44],
                                           1.0 / nel, 1e-5, ALU.mult, ALU.max))
        p.wait("tensor", "sV")
        banks.use(6)
        t(lambda: T.matmul(pbc[:, 0, 0:1], onesfr[:], stat[0:1, 44:45],
                           start=True, stop=True))
        p.wait("vector", "sE")
        vc(lambda: V.tensor_copy(stat[:, 45:46], pbc[:, 0, 0:1]))
        banks.consumed(6, "vector")
        vc(lambda: V.tensor_scalar(stat[:, 46:47], stat[:, 45:46], 0.5, None, ALU.mult))
        vc(lambda: V.tensor_scalar(stat[:, 47:48], stat[:, 46:47], -1.0, None, ALU.mult))
        vc(lambda wsc_idx=wsc_idx: V.tensor_copy(stat[:, 60 + wsc_idx:61 + wsc_idx], stat[:, 45:46]))
        p.wait("scalar", "sV")
        for ccx in range(n_cc):
            b_ = ccx % 2
            waitm("sync", "wslab%d" % b_)
            waitm("sync", "wslabS%d" % b_)
            p.run("sync", lambda ccx=ccx, b_=b_, free_n=free_n, w_name=w_name: SY.dma_start(
                out=wstage[:, b_, :free_n], in_=dt[w_name][ccx * 128:(ccx + 1) * 128, :]),
                wsem[b_], 16)
            wm[ccx] = (wsem[b_], p.mark(wsem[b_]))
            p.wait("scalar", *wm[ccx])
            waitm("scalar", "wadd%d" % b_)
            s(lambda b_=b_, free_n=free_n: S.activation(
                wsgn[:, 2 * b_, :free_n], wstage[:, b_, :free_n], AF.Sign,
                bias=stat[:, 47:48]))
            s(lambda b_=b_, free_n=free_n: S.activation(
                wsgn[:, 2 * b_ + 1, :free_n], wstage[:, b_, :free_n], AF.Sign,
                bias=stat[:, 46:47]))
            setmark("wslabS%d" % b_, "sS")
            p.wait("vector", "sS")
            v(lambda ccx=ccx, b_=b_, free_n=free_n, m_tile=m_tile: V.tensor_tensor(
                m_tile[:, ccx, :free_n], wsgn[:, 2 * b_, :free_n],
                wsgn[:, 2 * b_ + 1, :free_n], ALU.add))
            setmark("wadd%d" % b_, "sV")

    ternarize("w1t", CC_D, H4, m1, 0)
    ternarize("w2t", CC_H, D, m2, 1)
    if "wsc" in dbg:
        dbg_dump("wsc", stat[:, 56:64])
    if "m1c" in dbg:
        p.wait("vector", "sS")
        mdump = wsgn.rearrange("p a b -> p (a b)")
        for cc in range(CC_D):
            p.wait("vector", "sSt")
            v(lambda cc=cc: V.tensor_copy(mdump[:, :H4], m1[:, cc, :]))
            p.wait("sync", "sV")
            sdma(lambda cc=cc: SY.dma_start(out=dt["dbg_m1c"][:, cc], in_=mdump[:, :H4]))
    if "m2c" in dbg:
        p.wait("vector", "sS")
        mdump = wsgn.rearrange("p a b -> p (a b)")
        for cc in range(CC_H):
            p.wait("vector", "sSt")
            v(lambda cc=cc: V.tensor_copy(mdump[:, :D], m2[:, cc, :]))
            p.wait("sync", "sV")
            sdma(lambda cc=cc: SY.dma_start(out=dt["dbg_m2c"][:, cc], in_=mdump[:, :D]))
    p.join()
    stD.close(); OPEN_STACKS.remove(stD)

    stP2 = contextlib.ExitStack(); OPEN_STACKS.append(stP2)
    xt = sb(stP2, "xt", [128, 2, NSEG, D], BF16)
    rT = sb(stP2, "rT", [128, NSEG, D], BF16)
    bxT = sb(stP2, "bxT", [128, NSEG, D], BF16)
    wT = sb(stP2, "wT", [128, NSEG, D], BF16)
    x1T = sb(stP2, "x1T", [128, 2, NSEG, D], BF16)
    lowf = sb(stP2, "lowf", [128, CC_D, TTOK], BF16)
    lowT = sb(stP2, "lowT", [128, NSEG * CC_D, 128], BF16)
    n1T = lowT.rearrange("p a b -> p (a b)").rearrange("p (sg c) -> p sg c", sg=NSEG)
    n1f = sb(stP2, "n1f", [128, NSEG * CC_D, 128], BF16)
    sf32 = sb(stP2, "sf32", [128, 2048], F32)
    g4 = sb(stP2, "g4", [128, NSEG, H4], BF16)
    n2f = sb(stP2, "n2f", [128, CC_H, 128], BF16)
    hls2 = sb(stP2, "hls2", [128, 2, CC_D, 136], BF16)
    outT = sb(stP2, "outT", [128, 1, D], F32)
    st2 = sb(stP2, "st2", [128, 64], F32)
    sth2 = sb(stP2, "sth2", [128, 64], F32)

    rTf = rT.rearrange("p a b -> p (a b)")
    bxTf = bxT.rearrange("p a b -> p (a b)")
    wTf = wT.rearrange("p a b -> p (a b)")
    x1Tf = x1T.rearrange("p a b c -> p (a b c)")
    xtf = xt.rearrange("p a b c -> p (a b c)")
    g4f = g4.rearrange("p a b -> p (a b)")
    n1Tf = lowT.rearrange("p a b -> p (a b)")
    gbdf = gbd.rearrange("p a b -> p (a b)")
    gbhf = gbh.rearrange("p a b -> p (a b)")

    def seg(t2, sg, w):
        return t2[:, sg * w:(sg + 1) * w]

    n1ff = n1f.rearrange("p a b -> p (a b)")
    n2ff = n2f.rearrange("p a b -> p (a b)")
    lowff = lowf.rearrange("p a b -> p (a b)")
    outTf = outT.rearrange("p a b -> p (a b)")
    pm1 = pcv.rearrange("p a b -> p (a b)")
    def pbank(b):
        if b < 4:
            return pcv[:, b, :]
        if b < 6:
            return pkv[:, b - 4, :]
        return pbc[:, b - 6, :]

    tq1 = sf32[:, 0:1024]
    tq2 = sf32[:, 1024:2048]
    trh = sf32[:]
    th1 = sf32[:, 0:1024]
    th2 = sf32[:, 1024:2048]

    for s_ in ("sX0", "sX1", "sHL"):
        p.sems[s_] = st.enter_context(nc.semaphore(s_))
    xdma = [lambda fn: p.run("gpsimd", fn, "sX0", 16), lambda fn: p.run("gpsimd", fn, "sX1", 16)]
    hdma = lambda fn: p.run("sync", fn, "sHL", 16)

    C_M10 = consts[:, 3:4]

    def stats_d(src_ap, cbase, eng_sq_dst):
        vc(lambda: V.tensor_reduce(st2[:, cbase:cbase + 2], src_ap, AX.X, ALU.add))
        p.wait("scalar", "sV")
        for sg_ in range(NSEG):
            s(lambda sg_=sg_: S.activation(trh[:, sg_ * 1024:(sg_ + 1) * 1024],
                                           src_ap[:, sg_], AF.Square,
                                           accum_out=st2[:, cbase + 2 + sg_:cbase + 3 + sg_]))
        p.wait("vector", "sS")
        vc(lambda: V.scalar_tensor_tensor(st2[:, cbase + 4:cbase + 6], st2[:, cbase:cbase + 2],
                                         -(2.0 ** -20), st2[:, cbase:cbase + 2],
                                         ALU.mult, ALU.mult))
        vc(lambda: V.scalar_tensor_tensor(st2[:, cbase + 4:cbase + 6], st2[:, cbase + 2:cbase + 4],
                                         2.0 ** -10, st2[:, cbase + 4:cbase + 6],
                                         ALU.mult, ALU.add))
        vc(lambda: V.tensor_scalar(st2[:, cbase + 4:cbase + 6], st2[:, cbase + 4:cbase + 6],
                                  LN_EPS, None, ALU.add))

    def rstd_d(cbase):
        p.wait("scalar", "sV")
        s(lambda: S.activation(st2[:, cbase + 6:cbase + 8], st2[:, cbase + 4:cbase + 6], AF.Sqrt))
        p.wait("vector", "sS")
        vc(lambda: V.reciprocal(st2[:, cbase + 6:cbase + 8], st2[:, cbase + 6:cbase + 8]))
        vc(lambda: V.scalar_tensor_tensor(st2[:, cbase + 8:cbase + 10], st2[:, cbase:cbase + 2],
                                         2.0 ** -10, st2[:, cbase + 6:cbase + 8],
                                         ALU.mult, ALU.mult))

    def apply_ln(dst, src_ap, cbase, out_f32=False):
        p.wait("vector", "sS")
        for sg in range(NSEG):
            v(lambda sg=sg: V.tensor_scalar(dst[:, sg], src_ap[:, sg],
                                            st2[:, cbase + 6 + sg:cbase + 7 + sg],
                                            st2[:, cbase + 8 + sg:cbase + 9 + sg],
                                            ALU.mult, ALU.subtract))

    NT = NTILE
    for i in range(NT + 1):
        if i < NT:
            par = i % 2
            waitm("gpsimd", "xt_free_%d" % par)
            p.wait("gpsimd", "sTr")
            xdma[par](lambda i=i, par=par: G.dma_start(
                out=xt[:, par], in_=dt["xo"][i * TTOK:(i + 1) * TTOK, :].rearrange(
                    "(sg p) c -> p sg c", p=128)))
            xmark = p.mark("sX%d" % par)

            waitm("sync", "recon_done")
            def ld_h0(i=i):
                off = hf_val("sync") * 2048 + 128 * i + 3
                return SY.dma_start(out=hls2[:, 0, :, 0:130], in_=dt["h0hbm"][:, :, ds(off, 130)])
            def ld_lr(i=i):
                off = hf_val("sync") * 2048 + 128 * i + 3
                return SY.dma_start(out=hls2[:, 1, :, 0:130], in_=dt["lrhbm"][:, :, ds(off, 130)])
            hdma(ld_h0)
            hdma(ld_lr)
            p.wait("tensor", "sHL")
            for cc in range(CC_D):
                bank = 6 + cc % 2
                lrs = hls2[:, 1, cc]
                h0s = hls2[:, 0, cc]
                taps = [(dgrec[:, 0, cc], lrs[:, 0:128]),
                        (dgrec[:, 1, cc], lrs[:, 1:129]),
                        (dgrec[:, 2, cc], lrs[:, 2:130]),
                        (dgrec[:, 5, cc], h0s[:, 0:128]),
                        (dgrec[:, 6, cc], h0s[:, 1:129]),
                        (dgrec[:, 7, cc], h0s[:, 2:130])]
                conv_block(pbank(bank)[:, 0:128], taps, bank)
                taps = [(dgrec[:, 3, cc], lrs[:, 1:129]),
                        (dgrec[:, 4, cc], lrs[:, 2:130]),
                        (dgrec[:, 8, cc], h0s[:, 1:129]),
                        (dgrec[:, 9, cc], h0s[:, 2:130])]
                conv_block(pbank(bank)[:, 128:256], taps, bank)
                p.wait("scalar", "sE")
                p.wait("scalar", "sTr")
                s(lambda cc=cc, bank=bank: S.copy(lowf[:, cc, 0:256:2], pbank(bank)[:, 0:128]))
                s(lambda cc=cc, bank=bank: S.copy(lowf[:, cc, 1:256:2], pbank(bank)[:, 128:256]))
                banks.consumed(bank, "scalar")
            setmark("recon_done", "sE")
            tr_fence()
            p.wait("sync", "sS")
            waitm("sync", "rstt")
            trdma(lambda: SY.dma_start_transpose(lowT[:], lowff))
            setmark("lowT_rdy", "sTr")
            p.wait("vector", "sTr")
            if "lowt" in dbg:
                p.wait("sync", "sTr")
                sdma(lambda i=i: SY.dma_start(out=dt["dbg_lowt"][i], in_=lowT[:]))
                p.wait("vector", "sSt")
                p.wait("scalar", "sSt")

        if i > 0:
            parp = (i - 1) % 2
            for sg in range(NSEG):
                vc(lambda sg=sg: V.scalar_tensor_tensor(sth2[:, 4 + sg:5 + sg],
                                                 sth2[:, 0 + sg:1 + sg], -(2.0 ** -24),
                                                 sth2[:, 0 + sg:1 + sg], ALU.mult, ALU.mult))
                vc(lambda sg=sg: V.scalar_tensor_tensor(sth2[:, 4 + sg:5 + sg],
                                                 sth2[:, 2 + sg:3 + sg], 2.0 ** -12,
                                                 sth2[:, 4 + sg:5 + sg], ALU.mult, ALU.add))
                vc(lambda sg=sg: V.tensor_scalar(sth2[:, 4 + sg:5 + sg], sth2[:, 4 + sg:5 + sg],
                                                 LN_EPS, None, ALU.add))
                p.wait("scalar", "sV")
                s(lambda sg=sg: S.activation(sth2[:, 6 + sg:7 + sg], sth2[:, 4 + sg:5 + sg], AF.Sqrt))
                p.wait("vector", "sS")
                vc(lambda sg=sg: V.reciprocal(sth2[:, 6 + sg:7 + sg], sth2[:, 6 + sg:7 + sg]))
                vc(lambda sg=sg: V.scalar_tensor_tensor(sth2[:, 8 + sg:9 + sg],
                                                 sth2[:, 0 + sg:1 + sg], 2.0 ** -12,
                                                 sth2[:, 6 + sg:7 + sg], ALU.mult, ALU.mult))
                vc(lambda sg=sg: V.tensor_scalar(seg(g4f, sg, H4), seg(g4f, sg, H4),
                                                 sth2[:, 6 + sg:7 + sg],
                                                 sth2[:, 8 + sg:9 + sg], ALU.mult, ALU.subtract))
                v(lambda sg=sg: V.tensor_tensor(seg(g4f, sg, H4), seg(g4f, sg, H4), gbhf[:, 0:H4], ALU.mult))
                v(lambda sg=sg: V.tensor_tensor(seg(g4f, sg, H4), seg(g4f, sg, H4), gbhf[:, H4:2 * H4], ALU.add))
                if "l2t" in dbg:
                    p.wait("sync", "sV")
                    sdma(lambda i=i, sg=sg: SY.dma_start(out=dt["dbg_l2t"][i - 1, sg], in_=g4[:, sg]))
                    p.wait("vector", "sSt")
                    p.wait("scalar", "sSt")
                vc(lambda sg=sg: V.tensor_reduce(sth2[:, 10 + sg:11 + sg], seg(g4f, sg, H4), AX.X,
                                                 ALU.max, apply_absolute_value=True))
                vc(lambda sg=sg: V.tensor_scalar(sth2[:, 12 + sg:13 + sg],
                                                sth2[:, 10 + sg:11 + sg], 1e-5, None, ALU.max))
                vc(lambda sg=sg: V.reciprocal(sth2[:, 14 + sg:15 + sg], sth2[:, 12 + sg:13 + sg]))
                vc(lambda sg=sg: V.tensor_scalar(sth2[:, 16 + sg:17 + sg], sth2[:, 14 + sg:15 + sg],
                                                C_M10, 127.0, ALU.mult, ALU.mult))
                vc(lambda sg=sg: V.tensor_scalar(sth2[:, 18 + sg:19 + sg], sth2[:, 14 + sg:15 + sg],
                                                127.0 * S29, None, ALU.mult))
                vc(lambda sg=sg: V.tensor_scalar(sth2[:, 20 + sg:21 + sg], sth2[:, 12 + sg:13 + sg],
                                                stat[:, 61:62], 0.5 / 127.0, ALU.mult, ALU.mult))
                p.wait("vector", "sS")
                for hh in range(4):
                    sl = slice(sg * H4 + hh * 1024, sg * H4 + (hh + 1) * 1024)
                    v(lambda sl=sl, sg=sg: V.tensor_scalar(th1[:], g4f[:, sl],
                                                           sth2[:, 16 + sg:17 + sg], CMAG,
                                                           ALU.mult, ALU.add))
                    v(lambda sl=sl, sg=sg: V.tensor_scalar(th2[:], g4f[:, sl],
                                                           sth2[:, 18 + sg:19 + sg], CMAG,
                                                           ALU.mult, ALU.add))
                    v(lambda: V.tensor_scalar(th2[:], th2[:], P29, -CMAG52, ALU.mult, ALU.add))
                    v(lambda sl=sl: V.scalar_tensor_tensor(g4f[:, sl], th1[:], -CMAG,
                                                           th2[:], ALU.add, ALU.add))
                if "n2t" in dbg:
                    p.wait("sync", "sV")
                    sdma(lambda i=i, sg=sg: SY.dma_start(out=dt["dbg_n2t"][i - 1, sg], in_=g4[:, sg]))
                    p.wait("vector", "sSt")
                    p.wait("scalar", "sSt")
                setmark("n2codes_%d" % sg, "sV")
                tr_fence()
                waitm("sync", "mm2_prev")
                waitm("sync", "n2codes_%d" % sg)
                trdma(lambda sg=sg: SY.dma_start_transpose(n2f[:], g4[:, sg]))
                p.wait("tensor", "sTr")
                b0, b1 = (0, 1) if sg == 0 else (2, 3)
                banks.use(b0)
                banks.use(b1)
                for hc in range(CC_H):
                    t(lambda hc=hc, b0=b0: T.matmul(pbank(b0), n2f[:, hc], m2[:, hc, 0:512],
                                             start=(hc == 0), stop=(hc == CC_H - 1)))
                    t(lambda hc=hc, b1=b1: T.matmul(pbank(b1), n2f[:, hc], m2[:, hc, 512:1024],
                                             start=(hc == 0), stop=(hc == CC_H - 1)))
                setmark("mm2_prev", "sE")

        if i < NT:
            p.wait("vector", "sX%d" % par, xmark)
            for sg in range(NSEG):
                v(lambda sg=sg, par=par: V.scalar_tensor_tensor(
                    rT[:, sg], lowT[:, sg:16:2, :], 1.0, xt[:, par, sg], ALU.mult, ALU.add))
            setmark("rstt", "sV")
            if "rt" in dbg:
                p.wait("sync", "sV")
                sdma(lambda i=i: SY.dma_start(out=dt["dbg_rt"][i], in_=rT[:]))
                p.wait("vector", "sSt")
                p.wait("scalar", "sSt")
            stats_d(rT[:], 0, None)
        if i < NT:
            p.wait("scalar", "sV")
            rstd_d(0)
            p.wait("vector", "sS")
            for sg in range(NSEG):
                v(lambda sg=sg, par=par: V.tensor_tensor(seg(bxTf, sg, D), xtf[:, par * NSEG * D + sg * D:par * NSEG * D + sg * D + D], gbdf[:, D:2 * D], ALU.add))
            setmark("xt_free_%d" % par, "sV")
            apply_ln(rT[:], rT[:], 0)
            for sg in range(NSEG):
                v(lambda sg=sg: V.tensor_tensor(seg(rTf, sg, D), seg(rTf, sg, D), gbdf[:, 0:D], ALU.mult))
            for sg in range(NSEG):
                v(lambda sg=sg: V.tensor_tensor(seg(wTf, sg, D), seg(rTf, sg, D), seg(bxTf, sg, D), ALU.add))
            if "wt" in dbg:
                p.wait("sync", "sV")
                sdma(lambda i=i: SY.dma_start(out=dt["dbg_wt"][i], in_=wT[:]))
                p.wait("vector", "sSt")
                p.wait("scalar", "sSt")
            stats_d(wT[:], 10, None)
            rstd_d(10)
            apply_ln(x1T[:, par], wT[:], 10)
            if "x1t" in dbg:
                p.wait("sync", "sV")
                sdma(lambda i=i, par=par: SY.dma_start(out=dt["dbg_x1t"][i], in_=x1T[:, par]))
                p.wait("vector", "sSt")
                p.wait("scalar", "sSt")
            stats_d(x1T[:, par], 20, None)
            rstd_d(20)
            apply_ln(wT[:], x1T[:, par], 20)
            for sg in range(NSEG):
                v(lambda sg=sg: V.tensor_tensor(seg(wTf, sg, D), seg(wTf, sg, D), gbdf[:, 2 * D:3 * D], ALU.mult))
            for sg in range(NSEG):
                v(lambda sg=sg: V.tensor_tensor(seg(wTf, sg, D), seg(wTf, sg, D), gbdf[:, 3 * D:4 * D], ALU.add))
            for sg2 in range(NSEG):
                v(lambda sg2=sg2, par=par: V.tensor_tensor(
                    x1Tf[:, par * NSEG * D + sg2 * D:par * NSEG * D + sg2 * D + D],
                    x1Tf[:, par * NSEG * D + sg2 * D:par * NSEG * D + sg2 * D + D],
                    gbdf[:, 4 * D:5 * D], ALU.add))
            if "l1t" in dbg:
                p.wait("sync", "sV")
                sdma(lambda i=i: SY.dma_start(out=dt["dbg_l1t"][i], in_=wT[:]))
                p.wait("vector", "sSt")
                p.wait("scalar", "sSt")
            vc(lambda: V.tensor_reduce(st2[:, 28:30], wT[:], AX.X, ALU.max,
                                      apply_absolute_value=True))
            vc(lambda: V.tensor_scalar(st2[:, 30:32], st2[:, 28:30], 1e-5, None, ALU.max))
            vc(lambda: V.reciprocal(st2[:, 32:34], st2[:, 30:32]))
            vc(lambda: V.tensor_scalar(st2[:, 34:36], st2[:, 32:34], C_M10, 127.0,
                                      ALU.mult, ALU.mult))
            vc(lambda: V.tensor_scalar(st2[:, 36:38], st2[:, 32:34], 127.0 * S29, None,
                                      ALU.mult))
            vc(lambda: V.tensor_scalar(st2[:, 38:40], st2[:, 30:32], stat[:, 60:61], 0.5 / 127.0,
                                      ALU.mult, ALU.mult))
            p.wait("vector", "sS")
            for sg in range(NSEG):
                v(lambda sg=sg: V.tensor_scalar(tq1[:], seg(wTf, sg, D),
                                                st2[:, 34 + sg:35 + sg], CMAG, ALU.mult, ALU.add))
                v(lambda sg=sg: V.tensor_scalar(tq2[:], seg(wTf, sg, D),
                                                st2[:, 36 + sg:37 + sg], CMAG, ALU.mult, ALU.add))
                v(lambda: V.tensor_scalar(tq2[:], tq2[:], P29, -CMAG52, ALU.mult, ALU.add))
                v(lambda sg=sg: V.scalar_tensor_tensor(seg(n1Tf, sg, D), tq1[:], -CMAG, tq2[:],
                                                       ALU.add, ALU.add))
                setmark("n1c%d" % sg, "sV")
            if "n1t" in dbg:
                p.wait("sync", "sV")
                sdma(lambda i=i: SY.dma_start(out=dt["dbg_n1t"][i], in_=n1T[:]))
                p.wait("vector", "sSt")
                p.wait("scalar", "sSt")
            for sg in range(NSEG):
                tr_fence()
                waitm("sync", "mm1_done")
                waitm("sync", "n1c%d" % sg)
                trdma(lambda sg=sg: SY.dma_start_transpose(
                    n1f[:, sg * CC_D:(sg + 1) * CC_D, :], n1Tf[:, sg * 1024:(sg + 1) * 1024]))
                setmark("n1f_rdy%d" % sg, "sTr")

        if i > 0:
            parp = (i - 1) % 2
            for sg in range(NSEG):
                p.wait("vector", "sE")
                v(lambda sg=sg, parp=parp: V.scalar_tensor_tensor(
                    bxT[:, sg], pm1[:, sg * 1024:(sg + 1) * 1024],
                    sth2[:, 20 + sg:21 + sg], x1T[:, parp, sg], ALU.mult, ALU.add))
                banks.consumed(2 * sg, "vector")
                banks.consumed(2 * sg + 1, "vector")
            stats_d(bxT[:], 40, None)
            p.wait("scalar", "sV")
            s(lambda: S.activation(st2[:, 46:48], st2[:, 44:46], AF.Sqrt))
            p.wait("vector", "sS")
            vc(lambda: V.reciprocal(st2[:, 46:48], st2[:, 46:48]))
            vc(lambda: V.scalar_tensor_tensor(st2[:, 48:50], st2[:, 40:42], 2.0 ** -10,
                                             st2[:, 46:48], ALU.mult, ALU.mult))
            if "st2" in dbg:
                p.wait("sync", "sV")
                sdma(lambda i=i: SY.dma_start(out=dt["dbg_st2"][i - 1], in_=st2[:]))
                p.wait("vector", "sSt")
                p.wait("scalar", "sSt")
            for sg in range(NSEG):
                p.wait("vector", "sSt")
                v(lambda sg=sg: V.tensor_scalar(outT[:, 0], bxT[:, sg],
                                                st2[:, 46 + sg:47 + sg],
                                                st2[:, 48 + sg:49 + sg], ALU.mult, ALU.subtract))
                p.wait("sync", "sV")
                sdma(lambda i=i, sg=sg: SY.dma_start(
                    out=dt["y"][(i - 1) * TTOK + sg * 128:(i - 1) * TTOK + sg * 128 + 128, :],
                    in_=outT[:, 0]))
        if i < NT:
            for sg in range(NSEG):
                waitm("tensor", "n1f_rdy%d" % sg)
                for b in range(8):
                    banks.use(b)
                for cc in range(CC_D):
                    for hc in range(8):
                        t(lambda sg=sg, cc=cc, hc=hc: T.matmul(
                            pbank(hc), n1f[:, sg * CC_D + cc], m1[:, cc, hc * 512:(hc + 1) * 512],
                            start=(cc == 0), stop=(cc == CC_D - 1)))
                setmark("mm1_done", "sE")
                p.wait("scalar", "sE")
                p.wait("scalar", "sTr")
                for hc in range(8):
                    s(lambda sg=sg, hc=hc: S.activation(
                        g4[:, sg, hc * 512:(hc + 1) * 512], pbank(hc), AF.Copy,
                        scale=st2[:, 38 + sg:39 + sg]))
                    banks.consumed(hc, "scalar")
                p.wait("vector", "sS")
                v(lambda sg=sg: V.tensor_tensor(seg(g4f, sg, H4), seg(g4f, sg, H4), gbhf[:, 2 * H4:3 * H4], ALU.add))
                if "hpre" in dbg:
                    p.wait("sync", "sV")
                    sdma(lambda i=i, sg=sg: SY.dma_start(out=dt["dbg_hpre"][i, sg], in_=g4[:, sg]))
                    p.wait("vector", "sSt")
                    p.wait("scalar", "sSt")
                p.wait("scalar", "sV")
                s(lambda sg=sg: S.activation(g4[:, sg], g4[:, sg], AF.Gelu,
                                             accum_out=sth2[:, 0 + sg:1 + sg]))
                if "gt" in dbg:
                    p.wait("sync", "sS")
                    sdma(lambda i=i, sg=sg: SY.dma_start(out=dt["dbg_gt"][i, sg], in_=g4[:, sg]))
                    p.wait("vector", "sSt")
                    p.wait("scalar", "sSt")
                p.wait("scalar", "sV")
                s(lambda sg=sg: S.activation(trh[:], g4[:, sg, 0:2048], AF.Square,
                                             accum_out=sth2[:, 2 + sg:3 + sg]))
                s(lambda sg=sg: S.activation(trh[:], g4[:, sg, 2048:4096], AF.Square,
                                             accum_out=sth2[:, 24 + sg:25 + sg]))
                p.wait("vector", "sS")
                vc(lambda sg=sg: V.tensor_add(sth2[:, 2 + sg:3 + sg], sth2[:, 2 + sg:3 + sg],
                                             sth2[:, 24 + sg:25 + sg]))

    p.join()
    stP2.close(); OPEN_STACKS.remove(stP2)

    _emit_streams(blk, p)


def _emit_streams(blk, p):
    def runner(fns):
        def _f(_eng):
            for fn in fns:
                fn()
        return _f
    blk.sync(runner(p.streams["sync"]))
    blk.vector(runner(p.streams["vector"]))
    blk.scalar(runner(p.streams["scalar"]))
    blk.tensor(runner(p.streams["tensor"]))
    blk.gpsimd(runner(p.streams["gpsimd"]))


def host_inputs(inputs):
    return {
        "hfT": np.ascontiguousarray(np.asarray(inputs["L0_attn_h_filter"]).T),
        "gfT": np.ascontiguousarray(np.asarray(inputs["L0_attn_g_filter"]).T),
        "hsT": np.ascontiguousarray(np.asarray(inputs["L0_attn_h_synth"]).T),
        "gsT": np.ascontiguousarray(np.asarray(inputs["L0_attn_g_synth"]).T),
        "gna": np.asarray(inputs["L0_attn_norm_gamma"]),
        "bna": np.asarray(inputs["L0_attn_norm_beta"]),
        "g1": np.asarray(inputs["L0_ffn_ln1_g"]),
        "c1": np.asarray(inputs["L0_ffn_ln1_b"]),
        "b1": np.asarray(inputs["L0_ffn_b1"]),
        "g2": np.asarray(inputs["L0_ffn_ln2_g"]),
        "c2": np.asarray(inputs["L0_ffn_ln2_b"]),
        "b2": np.asarray(inputs["L0_ffn_b2"]),
        "w1t": np.ascontiguousarray(np.asarray(inputs["L0_ffn_w1"]).T),
        "w2t": np.ascontiguousarray(np.asarray(inputs["L0_ffn_w2"]).T),
    }


def kernel(**inputs):
    x = np.ascontiguousarray(np.asarray(inputs["L0_attn_x"], dtype=np.float32))
    nc = build_program()
    base = host_inputs(inputs)
    in_maps = []
    for core in range(8):
        b, hf = core // 2, core % 2
        im = dict(base)
        im["x"] = np.ascontiguousarray(x[b])
        im["xo"] = np.ascontiguousarray(x[b, hf * OWN:(hf + 1) * OWN])
        in_maps.append(im)
    res = run_bass_kernel_spmd(nc, in_maps, core_ids=list(range(8)))
    out = np.empty((4, S_SEQ, D), dtype=np.float32)
    for core in range(8):
        b, hf = core // 2, core % 2
        out[b, hf * OWN:(hf + 1) * OWN] = res.results[core]["y"]
    return out


# version 29 (final)
# speedup vs baseline: 1.0533x; 1.0533x over previous
import contextlib
import numpy as np
import concourse.bass as bass
import concourse.mybir as mybir
from concourse.bass_utils import run_bass_kernel_spmd

F32 = mybir.dt.float32
BF16 = mybir.dt.bfloat16
FP8 = mybir.dt.float8e4
AF = mybir.ActivationFunctionType
ALU = mybir.AluOpType
AX = mybir.AxisListType
ds = bass.ds

S_SEQ, D, H4 = 8192, 1024, 4096
OWN = 4096
NTILE, TTOK, NSEG = 16, 256, 2
CC_D, CC_H = 8, 32
LN_EPS = 1e-5
ATTN_EPS = 1e-6
CMAG = 1.5 * 2.0 ** 23
CMAG52 = 1.5 * 2.0 ** 52
S29 = 2.0 ** -29
P29 = 2.0 ** 29
JN = 2056
IN_ = 1032
AFULL = 2056
H0FULL = 4104
L1PAD = 4


class Prog:
    def __init__(self, nc):
        self.nc = nc
        self.streams = {e: [] for e in ("sync", "vector", "scalar", "tensor", "gpsimd")}
        self.cnt = {}
        self.sems = {}

    def emit(self, eng, fn):
        self.streams[eng].append(fn)

    def wait(self, eng, sem, val=None):
        v_ = self.cnt.get(sem, 0) if val is None else val
        if v_ <= 0:
            return
        s_, h = self.sems[sem], getattr(self.nc, eng)
        self.emit(eng, lambda: h.wait_ge(s_, v_))

    def mark(self, sem):
        return self.cnt.get(sem, 0)

    def run(self, eng, fn, inc=None, n=1):
        if inc is None:
            self.emit(eng, fn)
        else:
            s_ = self.sems[inc]
            self.cnt[inc] = self.cnt.get(inc, 0) + n
            self.emit(eng, lambda: fn().then_inc(s_, n))

    def join(self, skip=()):
        for eng in ("sync", "vector", "scalar", "tensor", "gpsimd"):
            for sem in ("sLd", "sTr", "sGp", "sSt", "sV", "sS", "sE", "sG", "sVr"):
                if sem not in skip:
                    self.wait(eng, sem)


ENG_SEM = {"vector": "sV", "scalar": "sS", "tensor": "sE", "gpsimd": "sG", "sync": "sLd"}


class Banks:

    def __init__(self, p):
        self.p = p
        self.last = {}

    def use(self, bank, producer_eng="tensor"):
        ev = self.last.get(bank)
        if ev is not None:
            sem, cnt = ev
            self.p.wait(producer_eng, sem, cnt)

    def consumed(self, bank, eng):
        sem = ENG_SEM[eng]
        self.last[bank] = (sem, self.p.mark(sem))


def build_program(dbg=()):
    nc = bass.Bass("TRN2", target_bir_lowering=False)
    dt = {}

    def din(name, shape, dtype=F32):
        dt[name] = nc.dram_tensor(name, shape, dtype, kind="ExternalInput").ap()

    def dout(name, shape, dtype=F32):
        dt[name] = nc.dram_tensor(name, shape, dtype, kind="ExternalOutput").ap()

    din("x", [S_SEQ, D])
    din("xo", [OWN, D])
    for n_ in ("hfT", "gfT", "hsT", "gsT"):
        din(n_, [D, 4])
    for n_ in ("gna", "bna", "g1", "c1", "b2"):
        din(n_, [D])
    for n_ in ("g2", "c2", "b1"):
        din(n_, [H4])
    din("w1t", [D, H4])
    din("w2t", [H4, D])
    dout("y", [OWN, D])
    dt["h0hbm"] = nc.dram_tensor("h0hbm", [128, CC_D, H0FULL], BF16, kind="Internal").ap()
    dt["lrhbm"] = nc.dram_tensor("lrhbm", [128, CC_D, H0FULL], BF16, kind="Internal").ap()

    dbg_shapes = {
        "low2": ([128, CC_D, 2048], BF16),
        "h0b": ([128, CC_D, H0FULL], BF16),
        "h1b": ([128, CC_D, AFULL], BF16),
        "qb": ([128, CC_D, 2048], BF16),
        "kvb": ([128, CC_D, 128], BF16),
        "zbd": ([128, CC_D, 16], BF16),
        "attn": ([128, CC_D, AFULL], BF16),
        "lrb": ([128, CC_D, H0FULL], BF16),
        "m1c": ([128, CC_D, H4], BF16),
        "m2c": ([128, CC_H, D], BF16),
        "wsc": ([128, 8], F32),
        "lowt": ([NTILE, 128, NSEG, D], BF16),
        "rt": ([NTILE, 128, NSEG, D], BF16),
        "wt": ([NTILE, 128, NSEG, D], BF16),
        "x1t": ([NTILE, 128, NSEG, D], BF16),
        "l1t": ([NTILE, 128, NSEG, D], BF16),
        "n1t": ([NTILE, 128, NSEG, D], BF16),
        "hpre": ([NTILE, NSEG, 128, H4], BF16),
        "gt": ([NTILE, NSEG, 128, H4], BF16),
        "l2t": ([NTILE, NSEG, 128, H4], BF16),
        "n2t": ([NTILE, NSEG, 128, H4], BF16),
        "st2": ([NTILE, 128, 64], F32),
    }
    for name in dbg:
        dout("dbg_" + name, *dbg_shapes[name])

    with contextlib.ExitStack() as st:
        p = Prog(nc)
        _build(nc, p, st, dt, set(dbg))
    return nc


def _build(nc, p, st, dt, dbg):
    def sb(stk, name, shape, dtype, side="left"):
        return stk.enter_context(nc.sbuf_tensor(name, shape, dtype, side=side))
    V, S, T, G, SY = nc.vector, nc.scalar, nc.tensor, nc.gpsimd, nc.sync

    fvec = sb(st, "fvec", [128, 4, CC_D, 4], F32)
    frec = sb(st, "frec", [128, 10, CC_D], F32)
    dmask = sb(st, "dmask", [128, 128], BF16)
    consts = sb(st, "consts", [128, 8], F32)
    selt = sb(st, "selt", [16, CC_D, 128], F32)
    zbd = sb(st, "zbd", [128, CC_D, 16], BF16)
    stat = sb(st, "stat", [128, 64], F32)
    stath = sb(st, "stath", [128, 64], F32)
    onesr = sb(st, "onesr", [1, 128], BF16)
    onec = sb(st, "onec", [128, 4], BF16)
    onesf = sb(st, "onesf", [128, 4], F32)
    onesfr = sb(st, "onesfr", [1, 128], F32)
    z8 = sb(st, "z8", [128, 8, 4], BF16)
    stI = contextlib.ExitStack()
    iosc = sb(stI, "iosc", [128, 8, 128], F32)

    blk = st.enter_context(nc.Block())
    for s_ in ("sLd", "sTr", "sGp", "sSt", "sV", "sS", "sE", "sG", "sVr"):
        p.sems[s_] = st.enter_context(nc.semaphore(s_))

    banks = Banks(p)
    marks = {}

    def setmark(name, sem):
        marks[name] = (sem, p.mark(sem))

    def waitm(eng, name):
        if name in marks:
            sem, val = marks[name]
            p.wait(eng, sem, val)
    _hfreg = {}

    def hf_val(eng_name):
        if eng_name not in _hfreg:
            eng = getattr(nc, eng_name)
            _hfreg[eng_name] = eng.compute_val(eng.partition_id() % 2)
        return _hfreg[eng_name]
    ldma = lambda fn: p.run("sync", fn, "sLd", 16)
    sdma = lambda fn: p.run("sync", fn, "sSt", 16)
    gdma = lambda fn: p.run("gpsimd", fn, "sGp", 16)
    trdma = lambda fn: p.run("sync", fn, "sTr", 16)

    def v(fn, ch=False):
        if ch:
            p.wait("vector", "sV")
        p.run("vector", fn, "sV")

    def vc(fn):
        v(fn, ch=True)

    def s(fn, ch=False):
        if ch:
            p.wait("scalar", "sS")
        p.run("scalar", fn, "sS")

    def t(fn, ch=False):
        if ch:
            p.wait("tensor", "sE")
        p.run("tensor", fn, "sE")

    def g(fn):
        p.wait("gpsimd", "sG")
        p.run("gpsimd", fn, "sG")

    def tr_fence(full=True):
        if full:
            for sem in ("sLd", "sGp", "sSt", "sHL", "sX0", "sX1"):
                if sem in p.sems:
                    p.wait("sync", sem)
        p.wait("sync", "sTr")

    def dbg_dump(name, src_ap):
        if name not in dbg:
            return
        for sem in ("sV", "sS", "sE", "sGp"):
            p.wait("sync", sem)
        sdma(lambda: SY.dma_start(out=dt["dbg_" + name][:], in_=src_ap))

    pcv = st.enter_context(nc.psum_tensor("pcv", [128, 4, 512], F32))
    pkv = st.enter_context(nc.psum_tensor("pkv", [128, 2, 512], F32))
    pbc = st.enter_context(nc.psum_tensor("pbc", [128, 2, 512], F32))

    def conv_block(ps_sl, taps, bank):
        banks.use(bank)
        for i, (dg, rhs) in enumerate(taps):
            t(lambda dg=dg, rhs=rhs, i=i, last=(i == len(taps) - 1):
              T.matmul(ps_sl, dg, rhs, start=(i == 0), stop=last))

    ldma(lambda: SY.dma_start(out=fvec[:, 0], in_=dt["hfT"].rearrange("(cc p) k -> p cc k", p=128)))
    ldma(lambda: SY.dma_start(out=fvec[:, 1], in_=dt["gfT"].rearrange("(cc p) k -> p cc k", p=128)))
    ldma(lambda: SY.dma_start(out=fvec[:, 2], in_=dt["hsT"].rearrange("(cc p) k -> p cc k", p=128)))
    ldma(lambda: SY.dma_start(out=fvec[:, 3], in_=dt["gsT"].rearrange("(cc p) k -> p cc k", p=128)))
    p.wait("vector", "sLd")

    for fi, base in ((2, 0), (3, 5)):
        vc(lambda fi=fi, base=base: V.tensor_copy(frec[:, base], fvec[:, fi, :, 0]))
        vc(lambda fi=fi, base=base: V.tensor_add(frec[:, base + 1], fvec[:, fi, :, 1], fvec[:, fi, :, 2]))
        vc(lambda fi=fi, base=base: V.tensor_copy(frec[:, base + 2], fvec[:, fi, :, 3]))
        vc(lambda fi=fi, base=base: V.tensor_add(frec[:, base + 3], fvec[:, fi, :, 0], fvec[:, fi, :, 1]))
        vc(lambda fi=fi, base=base: V.tensor_add(frec[:, base + 4], fvec[:, fi, :, 2], fvec[:, fi, :, 3]))

    p.run("gpsimd", lambda: G.iota(iosc[:, 0], [[1, 128]], base=0, channel_multiplier=-1, allow_small_or_imprecise_dtypes=True), "sG")
    p.run("gpsimd", lambda: G.iota(iosc[:16, 1], [[1, 128]], base=0, channel_multiplier=0, allow_small_or_imprecise_dtypes=True), "sG")
    p.run("gpsimd", lambda: G.iota(iosc[:16, 2], [[0, 128]], base=0, channel_multiplier=1, allow_small_or_imprecise_dtypes=True), "sG")
    p.wait("vector", "sG")
    vc(lambda: V.tensor_scalar(dmask[:], iosc[:, 0], 0.0, None, ALU.is_equal))
    vc(lambda: V.tensor_scalar(iosc[:16, 3], iosc[:16, 1], 63.0, None, ALU.is_gt))
    for cc in range(CC_D):
        vc(lambda cc=cc: V.tensor_scalar(iosc[:16, 4], iosc[:16, 3], float(2 * cc), None, ALU.add))
        vc(lambda cc=cc: V.tensor_tensor(selt[:, cc], iosc[:16, 4], iosc[:16, 2], ALU.is_equal))

    import os
    KCUT = int(os.environ.get("KDBG_CUT", "0"))
    class _CutDone(Exception):
        pass
    def cut(n):
        if KCUT == n:
            for sem in ("sV", "sS", "sE", "sG", "sGp", "sTr", "sLd"):
                p.wait("sync", sem)
            sdma(lambda: SY.dma_start(out=dt["y"][0:16, :128], in_=selt[:, 7]))
            p.wait("sync", "sSt")
            raise _CutDone()
    try:
        cut(1)
    except _CutDone:
        _emit_streams(blk, p)
        return
    OPEN_STACKS = []
    vc(lambda: V.memset(consts[:, 0:1], S29))
    vc(lambda: V.tensor_scalar(consts[:, 1:2], consts[:, 0:1], CMAG, -CMAG, ALU.add, ALU.add))
    vc(lambda: V.tensor_scalar(consts[:, 2:3], consts[:, 1:2], P29, None, ALU.mult))
    vc(lambda: V.tensor_scalar(consts[:, 3:4], consts[:, 2:3], -1.0, 1.0, ALU.mult, ALU.add))
    vc(lambda: V.memset(stat[:], 0.0))
    vc(lambda: V.memset(stath[:], 0.0))
    v(lambda: V.memset(onesr[:], 1.0))
    v(lambda: V.memset(onec[:], 1.0))
    v(lambda: V.memset(onesf[:], 1.0))
    v(lambda: V.memset(onesfr[:], 1.0))

    stI.close()
    stP1 = contextlib.ExitStack(); OPEN_STACKS.append(stP1)
    low2 = sb(stP1, "low2", [128, CC_D, 2112], BF16)
    h1b = sb(stP1, "h1b", [128, CC_D, 2112], BF16)

    stA = contextlib.ExitStack(); OPEN_STACKS.append(stA)
    xst = sb(stA, "xst", [128, CC_D, 10, 128], BF16)
    xf = sb(stA, "xf", [128, 2, 80, 128], BF16)
    low1w = sb(stA, "low1w", [128, CC_D, 1056], BF16)
    spst = sb(stA, "spst", [128, CC_D, 520], BF16)
    diagf = sb(stA, "diagf", [128, 8, CC_D, 128], BF16)

    v(lambda: V.memset(z8[:], 0.0))
    v(lambda: V.memset(xst[:, :, 0, :], 0.0))
    v(lambda: V.memset(spst[:], 0.0))
    setmark("xst0z", "sV")
    for fi in range(2):
        for k in range(4):
            for cc in range(CC_D):
                v(lambda fi=fi, k=k, cc=cc: V.tensor_scalar(
                    diagf[:, fi * 4 + k, cc], dmask[:], fvec[:, fi, cc, k:k + 1], None, ALU.mult))
    v(lambda: V.memset(low1w[:], 0.0))
    v(lambda: V.memset(low2[:], 0.0))
    v(lambda: V.memset(h1b[:], 0.0))

    try:
        cut(2)
    except _CutDone:
        _emit_streams(blk, p)
        for _s in reversed(OPEN_STACKS):
            _s.close()
        return
    BLK1 = L1PAD + 512
    for k in range(8):
        buf = k % 2
        lo = 1024 * k - 128
        p.wait("gpsimd", "sTr")
        p.wait("vector", "sTr")
        for r in range(10):
            tk = lo + r * 128
            if tk < 0 or tk >= S_SEQ:
                if k > 0:
                    v(lambda r=r: V.memset(xst[:, :, r, :], 0.0))
            else:
                gdma(lambda tk=tk, r=r: G.dma_start(
                    out=xst[:, :, r, :],
                    in_=dt["x"][tk:tk + 128, :].rearrange("p (cc c) -> p cc c", cc=CC_D)))
        p.wait("sync", "sGp")
        if k == 0:
            waitm("sync", "xst0z")
        elif k == 7:
            p.wait("sync", "sV")
        waitm("sync", "xf_used_%d" % buf)
        tr_fence()
        trdma(lambda buf=buf: SY.dma_start_transpose(
            xf[:, buf], xst.rearrange("p a r c -> p (a r c)")))
        p.wait("tensor", "sTr")
        xfb = xf[:, buf].rearrange("p a b -> p (a b)")
        if k == 0:
            p.wait("tensor", "sV")

        for cc in range(CC_D):
            bank = cc % 2
            taps = []
            for d_ in range(4):
                f0 = 128 + (d_ - 1)
                taps.append((diagf[:, d_, cc], xfb[:, cc * 1280 + f0: cc * 1280 + f0 + 1024:2]))
            conv_block(pcv[:, bank, :], taps, bank)
            p.wait("scalar", "sE")
            s(lambda cc=cc: S.copy(low1w[:, cc, BLK1:BLK1 + 512], pcv[:, cc % 2, :]))
            banks.consumed(bank, "scalar")
        for cc in range(CC_D):
            bank = 2 + cc % 2
            taps = []
            for d_ in range(4):
                f0 = 128 + (d_ - 1)
                taps.append((diagf[:, 4 + d_, cc], xfb[:, cc * 1280 + f0: cc * 1280 + f0 + 1024:2]))
            conv_block(pcv[:, bank, :], taps, bank)
            p.wait("vector", "sE")
            p.wait("vector", "sSt")
            v(lambda cc=cc: V.tensor_copy(spst[:, cc, 4:516], pcv[:, 2 + cc % 2, :512]))
            banks.consumed(bank, "vector")
        setmark("xf_used_%d" % buf, "sE")
        p.wait("sync", "sV")
        if k == 0:
            sdma(lambda: SY.dma_start(out=dt["h0hbm"][:, :, 0:516], in_=spst[:, :, 0:516]))
        elif k == 7:
            sdma(lambda: SY.dma_start(out=dt["h0hbm"][:, :, 4 + 512 * 7:H0FULL],
                                      in_=spst[:, :, 4:520]))
        else:
            sdma(lambda k=k: SY.dma_start(out=dt["h0hbm"][:, :, 4 + 512 * k:4 + 512 * k + 512],
                                          in_=spst[:, :, 4:516]))

        i0 = -2 if k == 0 else 256 * k - 2
        i1 = 2048 if k == 7 else 256 * (k + 1) - 2
        base_j = 512 * (k - 1)
        p.wait("tensor", "sS")
        for q0 in range(i0, i1, 512):
            qn = min(512, i1 - q0)
            for cc in range(CC_D):
                bank = cc % 2
                taps = []
                for d_ in range(4):
                    colf = 2 * q0 + d_ - 1 - base_j + L1PAD
                    taps.append((diagf[:, d_, cc], low1w[:, cc, colf:colf + 2 * qn:2]))
                conv_block(pcv[:, bank, :qn], taps, bank)
                p.wait("scalar", "sE")
                c0 = max(q0, 0)
                if c0 < q0 + qn:
                    s(lambda cc=cc, c0=c0, q0=q0, qn=qn: S.copy(
                        low2[:, cc, c0:q0 + qn], pcv[:, cc % 2, c0 - q0:qn]))
                banks.consumed(bank, "scalar")
            h0_ = max(q0, 0)
            hskip = h0_ - q0
            if h0_ < q0 + qn:
                for cc in range(CC_D):
                    bank = 2 + cc % 2
                    taps = []
                    for d_ in range(4):
                        colf = 2 * h0_ + d_ - 1 - base_j + L1PAD
                        taps.append((diagf[:, 4 + d_, cc], low1w[:, cc, colf:colf + 2 * (qn - hskip):2]))
                    conv_block(pcv[:, bank, :qn - hskip], taps, bank)
                    p.wait("vector", "sE")
                    v(lambda cc=cc, h0_=h0_, q0=q0, qn=qn, hskip=hskip: V.tensor_copy(
                        h1b[:, cc, h0_ + 4:q0 + 4 + qn], pcv[:, 2 + cc % 2, :qn - hskip]))
                    banks.consumed(bank, "vector")
        p.wait("vector", "sE")
        p.wait("vector", "sS")
        for cc in range(CC_D):
            v(lambda cc=cc: V.tensor_copy(low1w[:, cc, 0:516], low1w[:, cc, 512:1028]))

    try:
        cut(3)
    except _CutDone:
        _emit_streams(blk, p)
        for _s in reversed(OPEN_STACKS):
            _s.close()
        return
    dbg_dump("low2", low2[:, :, :2048])
    dbg_dump("h1b", h1b[:, :, :AFULL])
    p.join()
    stA.close(); OPEN_STACKS.remove(stA)

    attnb = sb(stP1, "attnb", [128, CC_D, 2112], BF16)
    stB = contextlib.ExitStack(); OPEN_STACKS.append(stB)
    qb = sb(stB, "qb", [128, CC_D, 2112], BF16)
    qTs = sb(stB, "qTs", [128, 2, 16, 128], BF16)
    vTs = sb(stB, "vTs", [128, 2, 16, 128], BF16)
    kvt = sb(stB, "kvt", [128, CC_D, 128], BF16)

    v(lambda: V.memset(attnb[:], 0.0))
    for cc in range(CC_D):
        v(lambda cc=cc: V.tensor_scalar(qb[:, cc, :2048], low2[:, cc, :2048], 0.0, None, ALU.min))
    p.wait("scalar", "sV")
    for cc in range(CC_D):
        s(lambda cc=cc: S.activation(qb[:, cc, :2048], qb[:, cc, :2048], AF.Exp))
    p.wait("vector", "sS")
    for cc in range(CC_D):
        v(lambda cc=cc: V.scalar_tensor_tensor(qb[:, cc, :2048], low2[:, cc, :2048], 0.0,
                                               qb[:, cc, :2048], ALU.max, ALU.add))
    dbg_dump("qb", qb[:, :, :2048])

    p.wait("sync", "sV")
    p.wait("sync", "sS")
    v(lambda: V.memset(kvt[:], 0.0))
    vc(lambda: V.memset(zbd[:], 0.0))
    p.wait("tensor", "sV")
    for cc in range(CC_D):
        b2_ = cc % 2
        p.wait("sync", "sE")
        tr_fence()
        trdma(lambda cc=cc, b2_=b2_: SY.dma_start_transpose(qTs[:, b2_], qb[:, cc, :2048]))
        tr_fence()
        trdma(lambda cc=cc, b2_=b2_: SY.dma_start_transpose(vTs[:, b2_], low2[:, cc, :2048]))
        p.wait("tensor", "sTr")
        bank = 4 + cc % 2
        banks.use(bank)
        for jn in range(16):
            t(lambda cc=cc, jn=jn, b2_=b2_: T.matmul(pkv[:, cc % 2, 0:128], qTs[:, b2_, jn],
                                                     vTs[:, b2_, jn],
                                                     start=(jn == 0), stop=(jn == 15)))
        for jn in range(16):
            t(lambda cc=cc, jn=jn, b2_=b2_: T.matmul(pkv[:, cc % 2, 128:129], qTs[:, b2_, jn],
                                                     onec[:, 0:1],
                                                     start=(jn == 0), stop=(jn == 15)))
        p.wait("vector", "sE")
        v(lambda cc=cc: V.tensor_copy(kvt[0:64, cc, 0:64], pkv[0:64, cc % 2, 0:64]))
        v(lambda cc=cc: V.tensor_copy(kvt[64:128, cc, 64:128], pkv[64:128, cc % 2, 64:128]))
        vc(lambda cc=cc: V.tensor_copy(zbd[0:64, cc, 2 * cc:2 * cc + 1], pkv[0:64, cc % 2, 128:129]))
        vc(lambda cc=cc: V.tensor_copy(zbd[64:128, cc, 2 * cc + 1:2 * cc + 2],
                                      pkv[64:128, cc % 2, 128:129]))
        banks.consumed(bank, "vector")
    dbg_dump("kvb", kvt[:])
    dbg_dump("zbd", zbd[:])

    rdent = sb(stB, "rdent", [16, 2048], F32)
    rden = rdent[:]
    p.wait("tensor", "sV")
    for q0 in range(0, 2048, 512):
        banks.use(6)
        for cc in range(CC_D):
            t(lambda cc=cc, q0=q0: T.matmul(
                pbc[0:16, 0, :], zbd[:, cc], qb[:, cc, q0:q0 + 512],
                start=(cc == 0), stop=(cc == CC_D - 1)))
        p.wait("vector", "sE")
        v(lambda q0=q0: V.tensor_scalar(rden[:, q0:q0 + 512], pbc[0:16, 0, :],
                                        ATTN_EPS, None, ALU.add))
        v(lambda q0=q0: V.reciprocal(rden[:, q0:q0 + 512], rden[:, q0:q0 + 512]))
        banks.consumed(6, "vector")

    bcs = sb(stB, "bcs", [128, 2, 512], BF16)
    p.wait("tensor", "sV")
    for cc in range(CC_D):
        for q0 in range(0, 2048, 512):
            par = (q0 // 512) % 2
            banks.use(6 + par)
            t(lambda cc=cc, q0=q0, par=par: T.matmul(pbc[:, par, :], selt[:, cc],
                                                     rden[:, q0:q0 + 512], start=True, stop=True))
            p.wait("scalar", "sE")
            p.wait("scalar", "sV", banks.last.get(("bcs", par), (None, 0))[1])
            s(lambda par=par: S.copy(bcs[:, par, :], pbc[:, par, :]))
            banks.consumed(6 + par, "scalar")
            banks.use(4 + par)
            t(lambda cc=cc, q0=q0, par=par: T.matmul(pkv[:, par, :], kvt[:, cc],
                                                     qb[:, cc, q0:q0 + 512], start=True, stop=True))
            p.wait("vector", "sE")
            p.wait("vector", "sS")
            v(lambda cc=cc, q0=q0, par=par: V.tensor_tensor(
                attnb[:, cc, 4 + q0:4 + q0 + 512], pkv[:, par, :], bcs[:, par, :], ALU.mult))
            banks.consumed(4 + par, "vector")
            banks.last[("bcs", par)] = ("sV", p.cnt.get("sV", 0))
    try:
        cut(4)
    except _CutDone:
        _emit_streams(blk, p)
        for _s in reversed(OPEN_STACKS):
            _s.close()
        return
    dbg_dump("attn", attnb[:, :, :AFULL])
    p.join()
    stB.close(); OPEN_STACKS.remove(stB)

    stC = contextlib.ExitStack(); OPEN_STACKS.append(stC)
    att_own = sb(stC, "att_own", [128, CC_D, 1040], BF16)
    h1_own = sb(stC, "h1_own", [128, CC_D, 1040], BF16)
    spst2 = sb(stC, "spst2", [128, CC_D, 1032], BF16)
    dgrec = sb(st, "dgrec", [128, 10, CC_D, 128], BF16, side="right")

    for ti in range(10):
        for cc in range(CC_D):
            v(lambda ti=ti, cc=cc: V.tensor_scalar(
                dgrec[:, ti, cc], dmask[:], frec[:, ti, cc:cc + 1], None, ALU.mult))

    DBG_STATIC_HF = -1
    if "lrb" in dbg:
        vc(lambda: V.memset(spst2[:], 0.0))
        p.wait("sync", "sV")
        for q0 in range(0, H0FULL, 1026):
            qn = min(1026, H0FULL - q0)
            sdma(lambda q0=q0, qn=qn: SY.dma_start(out=dt["lrhbm"][:, :, q0:q0 + qn],
                                                   in_=spst2[:, :, :qn]))
        p.wait("scalar", "sSt")

    def emit_own_copies():
        if DBG_STATIC_HF >= 0:
            off = DBG_STATIC_HF * 1024
            for cc in range(CC_D):
                V.tensor_copy(att_own[:, cc, 0:1036], attnb[:, cc, off:off + 1036])
                V.tensor_copy(h1_own[:, cc, 0:1036], h1b[:, cc, off:off + 1036])
            return
        off = hf_val("vector") * 1024
        for cc in range(CC_D):
            V.tensor_copy(att_own[:, cc, 0:1036], attnb[:, cc, ds(off, 1036)])
            V.tensor_copy(h1_own[:, cc, 0:1036], h1b[:, cc, ds(off, 1036)])
    p.emit("vector", emit_own_copies)
    vc(lambda: V.tensor_copy(stat[:, 0:1], consts[:, 0:1]))

    p.wait("scalar", "sV")
    p.wait("tensor", "sV")
    for jb in range(0, JN, 1024):
        je = min(jb + 1024, JN)
        n_e = (je - jb) // 2
        n_o = (je - jb) - n_e
        ue = 2 + jb // 2
        for cc in range(CC_D):
            bank = cc % 2
            taps = [(dgrec[:, 0, cc], att_own[:, cc, ue - 1:ue - 1 + n_e]),
                    (dgrec[:, 1, cc], att_own[:, cc, ue:ue + n_e]),
                    (dgrec[:, 2, cc], att_own[:, cc, ue + 1:ue + 1 + n_e]),
                    (dgrec[:, 5, cc], h1_own[:, cc, ue - 1:ue - 1 + n_e]),
                    (dgrec[:, 6, cc], h1_own[:, cc, ue:ue + n_e]),
                    (dgrec[:, 7, cc], h1_own[:, cc, ue + 1:ue + 1 + n_e])]
            conv_block(pcv[:, bank, :n_e], taps, bank)
            bank2 = 2 + cc % 2
            taps = [(dgrec[:, 3, cc], att_own[:, cc, ue:ue + n_o]),
                    (dgrec[:, 4, cc], att_own[:, cc, ue + 1:ue + 1 + n_o]),
                    (dgrec[:, 8, cc], h1_own[:, cc, ue:ue + n_o]),
                    (dgrec[:, 9, cc], h1_own[:, cc, ue + 1:ue + 1 + n_o])]
            conv_block(pcv[:, bank2, :n_o], taps, bank2)
            p.wait("scalar", "sE")
            p.wait("scalar", "sSt")
            s(lambda cc=cc, n_e=n_e: S.copy(spst2[:, cc, 0:2 * n_e:2], pcv[:, cc % 2, :n_e]))
            s(lambda cc=cc, n_o=n_o: S.copy(spst2[:, cc, 1:1 + 2 * n_o:2], pcv[:, 2 + cc % 2, :n_o]))
            banks.consumed(cc % 2, "scalar")
            banks.consumed(2 + cc % 2, "scalar")
        p.wait("sync", "sS")
        def spill_lr(jb=jb, je=je):
            off = hf_val("sync") * 2048 + jb
            return SY.dma_start(out=dt["lrhbm"][:, :, ds(off, je - jb)],
                                in_=spst2[:, :, :je - jb])
        sdma(spill_lr)

    p.wait("gpsimd", "sSt")
    gdma(lambda: G.dma_start(out=dt["lrhbm"][:, :, 0:4], in_=z8[:]))
    gdma(lambda: G.dma_start(out=dt["lrhbm"][:, :, H0FULL - 4:H0FULL], in_=z8[:]))
    if "lrb" in dbg:
        p.wait("sync", "sSt")
        ldma(lambda: SY.dma_start(out=dt["dbg_lrb"][:], in_=dt["lrhbm"][:]))
    if "h0b" in dbg:
        p.wait("sync", "sSt")
        ldma(lambda: SY.dma_start(out=dt["dbg_h0b"][:], in_=dt["h0hbm"][:]))
    p.join(skip=("sGp",))
    stC.close(); OPEN_STACKS.remove(stC)
    stP1.close(); OPEN_STACKS.remove(stP1)

    m1 = sb(st, "m1", [128, CC_D, H4], FP8, side="right")
    m2 = sb(st, "m2", [128, CC_H, D], FP8, side="right")
    gbd = sb(st, "gbd", [128, 5, D], BF16, side="right")
    gbh = sb(st, "gbh", [128, 3, H4], BF16, side="right")
    stD = contextlib.ExitStack(); OPEN_STACKS.append(stD)
    wstage = sb(stD, "wstage", [128, 2, 4096], F32)
    stDv = contextlib.ExitStack()
    vrow2 = sb(stDv, "vrow2", [1, 2 * 4224], F32)

    vdma = lambda fn: p.run("sync", fn, "sVr", 16)

    allv = [(gbd, 0, "gna"), (gbd, 1, "bna"), (gbd, 2, "g1"), (gbd, 3, "c1"),
            (gbd, 4, "b2"), (gbh, 0, "g2"), (gbh, 1, "c2"), (gbh, 2, "b1")]
    bqc = 0
    vwm = {}
    for idx, (dst3, slot, name) in enumerate(allv):
        nf = dt[name].shape[0]
        half = (idx % 2) * 4224
        waitm("sync", "vbc%d" % (idx % 2))
        vdma(lambda half=half, name=name, nf=nf: SY.dma_start(
            out=vrow2[:, half:half + nf], in_=dt[name][None, :]))
        p.wait("tensor", "sVr")
        for q in range(nf // 512):
            bank = 6 + bqc % 2
            bqc += 1
            banks.use(bank)
            t(lambda half=half, q=q, bank=bank: T.matmul(
                pbank(bank), onesfr[:], vrow2[:, half + q * 512:half + (q + 1) * 512],
                start=True, stop=True))
            p.wait("scalar", "sE")
            s(lambda q=q, dst3=dst3, slot=slot, bank=bank: S.copy(
                dst3[:, slot, q * 512:(q + 1) * 512], pbank(bank)))
            banks.consumed(bank, "scalar")
        setmark("vbc%d" % (idx % 2), "sE")
    stDv.close()
    wsgn = sb(stD, "wsgn", [128, 4, 4096], BF16)

    def ternarize(w_name, n_cc, free_n, m_tile, wsc_idx):
        for s_ in ("sW0", "sW1"):
            if s_ not in p.sems:
                p.sems[s_] = st.enter_context(nc.semaphore(s_))
        wsem = ["sW0", "sW1"]
        wm = {}
        for ccx in range(n_cc):
            b_ = ccx % 2
            waitm("sync", "wslab%d" % b_)
            waitm("sync", "wslabS%d" % b_)
            p.run("sync", lambda ccx=ccx, b_=b_, free_n=free_n, w_name=w_name: SY.dma_start(
                out=wstage[:, b_, :free_n], in_=dt[w_name][ccx * 128:(ccx + 1) * 128, :]),
                wsem[b_], 16)
            wm[ccx] = (wsem[b_], p.mark(wsem[b_]))
            p.wait("vector", *wm[ccx])
            vc(lambda ccx=ccx, b_=b_, free_n=free_n: V.tensor_reduce(
                stath[:, ccx:ccx + 1], wstage[:, b_, :free_n], AX.X, ALU.add,
                apply_absolute_value=True))
            setmark("wslab%d" % b_, "sV")
        vc(lambda n_cc=n_cc: V.tensor_reduce(stat[:, 42:43], stath[:, 0:n_cc], AX.X, ALU.add))
        p.wait("tensor", "sV")
        banks.use(6)
        t(lambda: T.matmul(pbc[0:1, 0, 0:1], stat[:, 42:43], onesf[:, 0:1],
                           start=True, stop=True))
        p.wait("vector", "sE")
        nel = n_cc * 128 * free_n
        vc(lambda: V.tensor_copy(stat[0:1, 43:44], pbc[0:1, 0, 0:1]))
        banks.consumed(6, "vector")
        vc(lambda nel=nel: V.tensor_scalar(stat[0:1, 44:45], stat[0:1, 43:44],
                                           1.0 / nel, 1e-5, ALU.mult, ALU.max))
        p.wait("tensor", "sV")
        banks.use(6)
        t(lambda: T.matmul(pbc[:, 0, 0:1], onesfr[:], stat[0:1, 44:45],
                           start=True, stop=True))
        p.wait("vector", "sE")
        vc(lambda: V.tensor_copy(stat[:, 45:46], pbc[:, 0, 0:1]))
        banks.consumed(6, "vector")
        vc(lambda: V.tensor_scalar(stat[:, 46:47], stat[:, 45:46], 0.5, None, ALU.mult))
        vc(lambda: V.tensor_scalar(stat[:, 47:48], stat[:, 46:47], -1.0, None, ALU.mult))
        vc(lambda wsc_idx=wsc_idx: V.tensor_copy(stat[:, 60 + wsc_idx:61 + wsc_idx], stat[:, 45:46]))
        p.wait("scalar", "sV")
        for ccx in range(n_cc):
            b_ = ccx % 2
            waitm("sync", "wslab%d" % b_)
            waitm("sync", "wslabS%d" % b_)
            p.run("sync", lambda ccx=ccx, b_=b_, free_n=free_n, w_name=w_name: SY.dma_start(
                out=wstage[:, b_, :free_n], in_=dt[w_name][ccx * 128:(ccx + 1) * 128, :]),
                wsem[b_], 16)
            wm[ccx] = (wsem[b_], p.mark(wsem[b_]))
            p.wait("scalar", *wm[ccx])
            waitm("scalar", "wadd%d" % b_)
            s(lambda b_=b_, free_n=free_n: S.activation(
                wsgn[:, 2 * b_, :free_n], wstage[:, b_, :free_n], AF.Sign,
                bias=stat[:, 47:48]))
            s(lambda b_=b_, free_n=free_n: S.activation(
                wsgn[:, 2 * b_ + 1, :free_n], wstage[:, b_, :free_n], AF.Sign,
                bias=stat[:, 46:47]))
            setmark("wslabS%d" % b_, "sS")
            p.wait("vector", "sS")
            v(lambda ccx=ccx, b_=b_, free_n=free_n, m_tile=m_tile: V.tensor_tensor(
                m_tile[:, ccx, :free_n], wsgn[:, 2 * b_, :free_n],
                wsgn[:, 2 * b_ + 1, :free_n], ALU.add))
            setmark("wadd%d" % b_, "sV")

    ternarize("w1t", CC_D, H4, m1, 0)
    ternarize("w2t", CC_H, D, m2, 1)
    if "wsc" in dbg:
        dbg_dump("wsc", stat[:, 56:64])
    if "m1c" in dbg:
        p.wait("vector", "sS")
        mdump = wsgn.rearrange("p a b -> p (a b)")
        for cc in range(CC_D):
            p.wait("vector", "sSt")
            v(lambda cc=cc: V.tensor_copy(mdump[:, :H4], m1[:, cc, :]))
            p.wait("sync", "sV")
            sdma(lambda cc=cc: SY.dma_start(out=dt["dbg_m1c"][:, cc], in_=mdump[:, :H4]))
    if "m2c" in dbg:
        p.wait("vector", "sS")
        mdump = wsgn.rearrange("p a b -> p (a b)")
        for cc in range(CC_H):
            p.wait("vector", "sSt")
            v(lambda cc=cc: V.tensor_copy(mdump[:, :D], m2[:, cc, :]))
            p.wait("sync", "sV")
            sdma(lambda cc=cc: SY.dma_start(out=dt["dbg_m2c"][:, cc], in_=mdump[:, :D]))
    p.join()
    stD.close(); OPEN_STACKS.remove(stD)

    stP2 = contextlib.ExitStack(); OPEN_STACKS.append(stP2)
    xt = sb(stP2, "xt", [128, 2, NSEG, D], BF16)
    rT = sb(stP2, "rT", [128, NSEG, D], BF16)
    bxT = sb(stP2, "bxT", [128, NSEG, D], BF16)
    wT = sb(stP2, "wT", [128, NSEG, D], BF16)
    x1T = sb(stP2, "x1T", [128, 2, NSEG, D], BF16)
    lowf = sb(stP2, "lowf", [128, CC_D, TTOK], BF16)
    lowT = sb(stP2, "lowT", [128, NSEG * CC_D, 128], BF16)
    n1T = lowT.rearrange("p a b -> p (a b)").rearrange("p (sg c) -> p sg c", sg=NSEG)
    n1f = sb(stP2, "n1f", [128, NSEG * CC_D, 128], BF16)
    sf32 = sb(stP2, "sf32", [128, 2048], F32)
    g4 = sb(stP2, "g4", [128, NSEG, H4], BF16)
    n2f = sb(stP2, "n2f", [128, CC_H, 128], BF16)
    hls2 = sb(stP2, "hls2", [128, 2, CC_D, 136], BF16)
    outT = sb(stP2, "outT", [128, 1, D], F32)
    st2 = sb(stP2, "st2", [128, 64], F32)
    sth2 = sb(stP2, "sth2", [128, 64], F32)

    rTf = rT.rearrange("p a b -> p (a b)")
    bxTf = bxT.rearrange("p a b -> p (a b)")
    wTf = wT.rearrange("p a b -> p (a b)")
    x1Tf = x1T.rearrange("p a b c -> p (a b c)")
    xtf = xt.rearrange("p a b c -> p (a b c)")
    g4f = g4.rearrange("p a b -> p (a b)")
    n1Tf = lowT.rearrange("p a b -> p (a b)")
    gbdf = gbd.rearrange("p a b -> p (a b)")
    gbhf = gbh.rearrange("p a b -> p (a b)")

    def seg(t2, sg, w):
        return t2[:, sg * w:(sg + 1) * w]

    n1ff = n1f.rearrange("p a b -> p (a b)")
    n2ff = n2f.rearrange("p a b -> p (a b)")
    lowff = lowf.rearrange("p a b -> p (a b)")
    outTf = outT.rearrange("p a b -> p (a b)")
    pm1 = pcv.rearrange("p a b -> p (a b)")
    def pbank(b):
        if b < 4:
            return pcv[:, b, :]
        if b < 6:
            return pkv[:, b - 4, :]
        return pbc[:, b - 6, :]

    tq1 = sf32[:, 0:1024]
    tq2 = sf32[:, 1024:2048]
    trh = sf32[:]
    th1 = sf32[:, 0:1024]
    th2 = sf32[:, 1024:2048]

    for s_ in ("sX0", "sX1", "sHL"):
        p.sems[s_] = st.enter_context(nc.semaphore(s_))
    xdma = [lambda fn: p.run("gpsimd", fn, "sX0", 16), lambda fn: p.run("gpsimd", fn, "sX1", 16)]
    hdma = lambda fn: p.run("sync", fn, "sHL", 16)

    C_M10 = consts[:, 3:4]

    def stats_d(src_ap, cbase, eng_sq_dst):
        vc(lambda: V.tensor_reduce(st2[:, cbase:cbase + 2], src_ap, AX.X, ALU.add))
        p.wait("scalar", "sV")
        p.wait("scalar", "sSt")
        for sg_ in range(NSEG):
            s(lambda sg_=sg_: S.activation(trh[:, sg_ * 1024:(sg_ + 1) * 1024],
                                           src_ap[:, sg_], AF.Square,
                                           accum_out=st2[:, cbase + 2 + sg_:cbase + 3 + sg_]))
        p.wait("vector", "sS")
        vc(lambda: V.scalar_tensor_tensor(st2[:, cbase + 4:cbase + 6], st2[:, cbase:cbase + 2],
                                         -(2.0 ** -20), st2[:, cbase:cbase + 2],
                                         ALU.mult, ALU.mult))
        vc(lambda: V.scalar_tensor_tensor(st2[:, cbase + 4:cbase + 6], st2[:, cbase + 2:cbase + 4],
                                         2.0 ** -10, st2[:, cbase + 4:cbase + 6],
                                         ALU.mult, ALU.add))
        vc(lambda: V.tensor_scalar(st2[:, cbase + 4:cbase + 6], st2[:, cbase + 4:cbase + 6],
                                  LN_EPS, None, ALU.add))

    def rstd_d(cbase):
        p.wait("scalar", "sV")
        s(lambda: S.activation(st2[:, cbase + 6:cbase + 8], st2[:, cbase + 4:cbase + 6], AF.Sqrt))
        p.wait("vector", "sS")
        vc(lambda: V.reciprocal(st2[:, cbase + 6:cbase + 8], st2[:, cbase + 6:cbase + 8]))
        vc(lambda: V.scalar_tensor_tensor(st2[:, cbase + 8:cbase + 10], st2[:, cbase:cbase + 2],
                                         2.0 ** -10, st2[:, cbase + 6:cbase + 8],
                                         ALU.mult, ALU.mult))

    def apply_ln(dst, src_ap, cbase, out_f32=False):
        p.wait("vector", "sS")
        for sg in range(NSEG):
            v(lambda sg=sg: V.tensor_scalar(dst[:, sg], src_ap[:, sg],
                                            st2[:, cbase + 6 + sg:cbase + 7 + sg],
                                            st2[:, cbase + 8 + sg:cbase + 9 + sg],
                                            ALU.mult, ALU.subtract))

    NT = NTILE
    for i in range(NT + 1):
        if i < NT:
            par = i % 2
            waitm("gpsimd", "xt_free_%d" % par)
            p.wait("gpsimd", "sTr")
            xdma[par](lambda i=i, par=par: G.dma_start(
                out=xt[:, par], in_=dt["xo"][i * TTOK:(i + 1) * TTOK, :].rearrange(
                    "(sg p) c -> p sg c", p=128)))
            xmark = p.mark("sX%d" % par)

            waitm("sync", "recon_done")
            def ld_h0(i=i):
                off = hf_val("sync") * 2048 + 128 * i + 3
                return SY.dma_start(out=hls2[:, 0, :, 0:130], in_=dt["h0hbm"][:, :, ds(off, 130)])
            def ld_lr(i=i):
                off = hf_val("sync") * 2048 + 128 * i + 3
                return SY.dma_start(out=hls2[:, 1, :, 0:130], in_=dt["lrhbm"][:, :, ds(off, 130)])
            hdma(ld_h0)
            hdma(ld_lr)
            p.wait("tensor", "sHL")
            for cc in range(CC_D):
                bank = 6 + cc % 2
                lrs = hls2[:, 1, cc]
                h0s = hls2[:, 0, cc]
                taps = [(dgrec[:, 0, cc], lrs[:, 0:128]),
                        (dgrec[:, 1, cc], lrs[:, 1:129]),
                        (dgrec[:, 2, cc], lrs[:, 2:130]),
                        (dgrec[:, 5, cc], h0s[:, 0:128]),
                        (dgrec[:, 6, cc], h0s[:, 1:129]),
                        (dgrec[:, 7, cc], h0s[:, 2:130])]
                conv_block(pbank(bank)[:, 0:128], taps, bank)
                taps = [(dgrec[:, 3, cc], lrs[:, 1:129]),
                        (dgrec[:, 4, cc], lrs[:, 2:130]),
                        (dgrec[:, 8, cc], h0s[:, 1:129]),
                        (dgrec[:, 9, cc], h0s[:, 2:130])]
                conv_block(pbank(bank)[:, 128:256], taps, bank)
                p.wait("scalar", "sE")
                p.wait("scalar", "sTr")
                s(lambda cc=cc, bank=bank: S.copy(lowf[:, cc, 0:256:2], pbank(bank)[:, 0:128]))
                s(lambda cc=cc, bank=bank: S.copy(lowf[:, cc, 1:256:2], pbank(bank)[:, 128:256]))
                banks.consumed(bank, "scalar")
            setmark("recon_done", "sE")
            tr_fence()
            p.wait("sync", "sS")
            waitm("sync", "rstt")
            trdma(lambda: SY.dma_start_transpose(lowT[:], lowff))
            setmark("lowT_rdy", "sTr")
            p.wait("vector", "sTr")
            if "lowt" in dbg:
                p.wait("sync", "sTr")
                sdma(lambda i=i: SY.dma_start(out=dt["dbg_lowt"][i], in_=lowT[:]))
                p.wait("vector", "sSt")
                p.wait("scalar", "sSt")

        if i > 0:
            parp = (i - 1) % 2
            for sg in range(NSEG):
                vc(lambda sg=sg: V.scalar_tensor_tensor(sth2[:, 4 + sg:5 + sg],
                                                 sth2[:, 0 + sg:1 + sg], -(2.0 ** -24),
                                                 sth2[:, 0 + sg:1 + sg], ALU.mult, ALU.mult))
                vc(lambda sg=sg: V.scalar_tensor_tensor(sth2[:, 4 + sg:5 + sg],
                                                 sth2[:, 2 + sg:3 + sg], 2.0 ** -12,
                                                 sth2[:, 4 + sg:5 + sg], ALU.mult, ALU.add))
                vc(lambda sg=sg: V.tensor_scalar(sth2[:, 4 + sg:5 + sg], sth2[:, 4 + sg:5 + sg],
                                                 LN_EPS, None, ALU.add))
                p.wait("scalar", "sV")
                s(lambda sg=sg: S.activation(sth2[:, 6 + sg:7 + sg], sth2[:, 4 + sg:5 + sg], AF.Sqrt))
                p.wait("vector", "sS")
                vc(lambda sg=sg: V.reciprocal(sth2[:, 6 + sg:7 + sg], sth2[:, 6 + sg:7 + sg]))
                vc(lambda sg=sg: V.scalar_tensor_tensor(sth2[:, 8 + sg:9 + sg],
                                                 sth2[:, 0 + sg:1 + sg], 2.0 ** -12,
                                                 sth2[:, 6 + sg:7 + sg], ALU.mult, ALU.mult))
                vc(lambda sg=sg: V.tensor_scalar(seg(g4f, sg, H4), seg(g4f, sg, H4),
                                                 sth2[:, 6 + sg:7 + sg],
                                                 sth2[:, 8 + sg:9 + sg], ALU.mult, ALU.subtract))
                v(lambda sg=sg: V.tensor_tensor(seg(g4f, sg, H4), seg(g4f, sg, H4), gbhf[:, 0:H4], ALU.mult))
                v(lambda sg=sg: V.tensor_tensor(seg(g4f, sg, H4), seg(g4f, sg, H4), gbhf[:, H4:2 * H4], ALU.add))
                if "l2t" in dbg:
                    p.wait("sync", "sV")
                    sdma(lambda i=i, sg=sg: SY.dma_start(out=dt["dbg_l2t"][i - 1, sg], in_=g4[:, sg]))
                    p.wait("vector", "sSt")
                    p.wait("scalar", "sSt")
                vc(lambda sg=sg: V.tensor_reduce(sth2[:, 10 + sg:11 + sg], seg(g4f, sg, H4), AX.X,
                                                 ALU.max, apply_absolute_value=True))
                vc(lambda sg=sg: V.tensor_scalar(sth2[:, 12 + sg:13 + sg],
                                                sth2[:, 10 + sg:11 + sg], 1e-5, None, ALU.max))
                vc(lambda sg=sg: V.reciprocal(sth2[:, 14 + sg:15 + sg], sth2[:, 12 + sg:13 + sg]))
                vc(lambda sg=sg: V.tensor_scalar(sth2[:, 16 + sg:17 + sg], sth2[:, 14 + sg:15 + sg],
                                                C_M10, 127.0, ALU.mult, ALU.mult))
                vc(lambda sg=sg: V.tensor_scalar(sth2[:, 18 + sg:19 + sg], sth2[:, 14 + sg:15 + sg],
                                                127.0 * S29, None, ALU.mult))
                vc(lambda sg=sg: V.tensor_scalar(sth2[:, 20 + sg:21 + sg], sth2[:, 12 + sg:13 + sg],
                                                stat[:, 61:62], 0.5 / 127.0, ALU.mult, ALU.mult))
                p.wait("vector", "sS")
                p.wait("vector", "sSt")
                for hh in range(4):
                    sl = slice(sg * H4 + hh * 1024, sg * H4 + (hh + 1) * 1024)
                    v(lambda sl=sl, sg=sg: V.tensor_scalar(th1[:], g4f[:, sl],
                                                           sth2[:, 16 + sg:17 + sg], CMAG,
                                                           ALU.mult, ALU.add))
                    v(lambda sl=sl, sg=sg: V.tensor_scalar(th2[:], g4f[:, sl],
                                                           sth2[:, 18 + sg:19 + sg], CMAG,
                                                           ALU.mult, ALU.add))
                    v(lambda: V.tensor_scalar(th2[:], th2[:], P29, -CMAG52, ALU.mult, ALU.add))
                    v(lambda sl=sl: V.scalar_tensor_tensor(g4f[:, sl], th1[:], -CMAG,
                                                           th2[:], ALU.add, ALU.add))
                if "n2t" in dbg:
                    p.wait("sync", "sV")
                    sdma(lambda i=i, sg=sg: SY.dma_start(out=dt["dbg_n2t"][i - 1, sg], in_=g4[:, sg]))
                    p.wait("vector", "sSt")
                    p.wait("scalar", "sSt")
                setmark("n2codes_%d" % sg, "sV")
                tr_fence()
                waitm("sync", "mm2_prev")
                waitm("sync", "n2codes_%d" % sg)
                trdma(lambda sg=sg: SY.dma_start_transpose(n2f[:], g4[:, sg]))
                p.wait("tensor", "sTr")
                b0, b1 = (0, 1) if sg == 0 else (2, 3)
                banks.use(b0)
                banks.use(b1)
                for hc in range(CC_H):
                    t(lambda hc=hc, b0=b0: T.matmul(pbank(b0), n2f[:, hc], m2[:, hc, 0:512],
                                             start=(hc == 0), stop=(hc == CC_H - 1)))
                    t(lambda hc=hc, b1=b1: T.matmul(pbank(b1), n2f[:, hc], m2[:, hc, 512:1024],
                                             start=(hc == 0), stop=(hc == CC_H - 1)))
                setmark("mm2_prev", "sE")

        if i < NT:
            p.wait("vector", "sX%d" % par, xmark)
            for sg in range(NSEG):
                v(lambda sg=sg, par=par: V.scalar_tensor_tensor(
                    rT[:, sg], lowT[:, sg:16:2, :], 1.0, xt[:, par, sg], ALU.mult, ALU.add))
            setmark("rstt", "sV")
            if "rt" in dbg:
                p.wait("sync", "sV")
                sdma(lambda i=i: SY.dma_start(out=dt["dbg_rt"][i], in_=rT[:]))
                p.wait("vector", "sSt")
                p.wait("scalar", "sSt")
            stats_d(rT[:], 0, None)
        if i < NT:
            p.wait("scalar", "sV")
            rstd_d(0)
            p.wait("vector", "sS")
            for sg in range(NSEG):
                v(lambda sg=sg, par=par: V.tensor_tensor(seg(bxTf, sg, D), xtf[:, par * NSEG * D + sg * D:par * NSEG * D + sg * D + D], gbdf[:, D:2 * D], ALU.add))
            setmark("xt_free_%d" % par, "sV")
            apply_ln(rT[:], rT[:], 0)
            for sg in range(NSEG):
                v(lambda sg=sg: V.tensor_tensor(seg(rTf, sg, D), seg(rTf, sg, D), gbdf[:, 0:D], ALU.mult))
            for sg in range(NSEG):
                v(lambda sg=sg: V.tensor_tensor(seg(wTf, sg, D), seg(rTf, sg, D), seg(bxTf, sg, D), ALU.add))
            if "wt" in dbg:
                p.wait("sync", "sV")
                sdma(lambda i=i: SY.dma_start(out=dt["dbg_wt"][i], in_=wT[:]))
                p.wait("vector", "sSt")
                p.wait("scalar", "sSt")
            stats_d(wT[:], 10, None)
            rstd_d(10)
            apply_ln(x1T[:, par], wT[:], 10)
            if "x1t" in dbg:
                p.wait("sync", "sV")
                sdma(lambda i=i, par=par: SY.dma_start(out=dt["dbg_x1t"][i], in_=x1T[:, par]))
                p.wait("vector", "sSt")
                p.wait("scalar", "sSt")
            stats_d(x1T[:, par], 20, None)
            rstd_d(20)
            apply_ln(wT[:], x1T[:, par], 20)
            for sg in range(NSEG):
                v(lambda sg=sg: V.tensor_tensor(seg(wTf, sg, D), seg(wTf, sg, D), gbdf[:, 2 * D:3 * D], ALU.mult))
            for sg in range(NSEG):
                v(lambda sg=sg: V.tensor_tensor(seg(wTf, sg, D), seg(wTf, sg, D), gbdf[:, 3 * D:4 * D], ALU.add))
            for sg2 in range(NSEG):
                v(lambda sg2=sg2, par=par: V.tensor_tensor(
                    x1Tf[:, par * NSEG * D + sg2 * D:par * NSEG * D + sg2 * D + D],
                    x1Tf[:, par * NSEG * D + sg2 * D:par * NSEG * D + sg2 * D + D],
                    gbdf[:, 4 * D:5 * D], ALU.add))
            if "l1t" in dbg:
                p.wait("sync", "sV")
                sdma(lambda i=i: SY.dma_start(out=dt["dbg_l1t"][i], in_=wT[:]))
                p.wait("vector", "sSt")
                p.wait("scalar", "sSt")
            vc(lambda: V.tensor_reduce(st2[:, 28:30], wT[:], AX.X, ALU.max,
                                      apply_absolute_value=True))
            vc(lambda: V.tensor_scalar(st2[:, 30:32], st2[:, 28:30], 1e-5, None, ALU.max))
            vc(lambda: V.reciprocal(st2[:, 32:34], st2[:, 30:32]))
            vc(lambda: V.tensor_scalar(st2[:, 34:36], st2[:, 32:34], C_M10, 127.0,
                                      ALU.mult, ALU.mult))
            vc(lambda: V.tensor_scalar(st2[:, 36:38], st2[:, 32:34], 127.0 * S29, None,
                                      ALU.mult))
            vc(lambda: V.tensor_scalar(st2[:, 38:40], st2[:, 30:32], stat[:, 60:61], 0.5 / 127.0,
                                      ALU.mult, ALU.mult))
            p.wait("vector", "sS")
            p.wait("vector", "sSt")
            for sg in range(NSEG):
                v(lambda sg=sg: V.tensor_scalar(tq1[:], seg(wTf, sg, D),
                                                st2[:, 34 + sg:35 + sg], CMAG, ALU.mult, ALU.add))
                v(lambda sg=sg: V.tensor_scalar(tq2[:], seg(wTf, sg, D),
                                                st2[:, 36 + sg:37 + sg], CMAG, ALU.mult, ALU.add))
                v(lambda: V.tensor_scalar(tq2[:], tq2[:], P29, -CMAG52, ALU.mult, ALU.add))
                v(lambda sg=sg: V.scalar_tensor_tensor(seg(n1Tf, sg, D), tq1[:], -CMAG, tq2[:],
                                                       ALU.add, ALU.add))
            setmark("n1codes", "sV")
            if "n1t" in dbg:
                p.wait("sync", "sV")
                sdma(lambda i=i: SY.dma_start(out=dt["dbg_n1t"][i], in_=n1T[:]))
                p.wait("vector", "sSt")
                p.wait("scalar", "sSt")
            tr_fence()
            waitm("sync", "mm1_done")
            waitm("sync", "n1codes")
            trdma(lambda: SY.dma_start_transpose(n1f[:], n1Tf))
            setmark("n1f_rdy", "sTr")

        if i > 0:
            parp = (i - 1) % 2
            for sg in range(NSEG):
                p.wait("vector", "sE")
                v(lambda sg=sg, parp=parp: V.scalar_tensor_tensor(
                    bxT[:, sg], pm1[:, sg * 1024:(sg + 1) * 1024],
                    sth2[:, 20 + sg:21 + sg], x1T[:, parp, sg], ALU.mult, ALU.add))
                banks.consumed(2 * sg, "vector")
                banks.consumed(2 * sg + 1, "vector")
            stats_d(bxT[:], 40, None)
            p.wait("scalar", "sV")
            s(lambda: S.activation(st2[:, 46:48], st2[:, 44:46], AF.Sqrt))
            p.wait("vector", "sS")
            vc(lambda: V.reciprocal(st2[:, 46:48], st2[:, 46:48]))
            vc(lambda: V.scalar_tensor_tensor(st2[:, 48:50], st2[:, 40:42], 2.0 ** -10,
                                             st2[:, 46:48], ALU.mult, ALU.mult))
            if "st2" in dbg:
                p.wait("sync", "sV")
                sdma(lambda i=i: SY.dma_start(out=dt["dbg_st2"][i - 1], in_=st2[:]))
                p.wait("vector", "sSt")
                p.wait("scalar", "sSt")
            p.wait("vector", "sSt")
            v(lambda: V.tensor_scalar(outT[:, 0], bxT[:, 0],
                                      st2[:, 46:47], st2[:, 48:49],
                                      ALU.mult, ALU.subtract))
            v(lambda: V.tensor_scalar(sf32[:, 0:1024], bxT[:, 1],
                                      st2[:, 47:48], st2[:, 49:50],
                                      ALU.mult, ALU.subtract))
            p.wait("sync", "sV")
            sdma(lambda i=i: SY.dma_start(
                out=dt["y"][(i - 1) * TTOK:(i - 1) * TTOK + 128, :], in_=outT[:, 0]))
            sdma(lambda i=i: SY.dma_start(
                out=dt["y"][(i - 1) * TTOK + 128:(i - 1) * TTOK + 256, :],
                in_=sf32[:, 0:1024]))
        if i < NT:
            p.wait("tensor", "sTr")
            for sg in range(NSEG):
                for b in range(8):
                    banks.use(b)
                for cc in range(CC_D):
                    for hc in range(8):
                        t(lambda sg=sg, cc=cc, hc=hc: T.matmul(
                            pbank(hc), n1f[:, sg * CC_D + cc], m1[:, cc, hc * 512:(hc + 1) * 512],
                            start=(cc == 0), stop=(cc == CC_D - 1)))
                setmark("mm1_done", "sE")
                p.wait("scalar", "sE")
                p.wait("scalar", "sTr")
                for hc in range(8):
                    s(lambda sg=sg, hc=hc: S.activation(
                        g4[:, sg, hc * 512:(hc + 1) * 512], pbank(hc), AF.Copy,
                        scale=st2[:, 38 + sg:39 + sg]))
                    banks.consumed(hc, "scalar")
                p.wait("vector", "sS")
                v(lambda sg=sg: V.tensor_tensor(seg(g4f, sg, H4), seg(g4f, sg, H4), gbhf[:, 2 * H4:3 * H4], ALU.add))
                if "hpre" in dbg:
                    p.wait("sync", "sV")
                    sdma(lambda i=i, sg=sg: SY.dma_start(out=dt["dbg_hpre"][i, sg], in_=g4[:, sg]))
                    p.wait("vector", "sSt")
                    p.wait("scalar", "sSt")
                p.wait("scalar", "sV")
                s(lambda sg=sg: S.activation(g4[:, sg], g4[:, sg], AF.Gelu,
                                             accum_out=sth2[:, 0 + sg:1 + sg]))
                if "gt" in dbg:
                    p.wait("sync", "sS")
                    sdma(lambda i=i, sg=sg: SY.dma_start(out=dt["dbg_gt"][i, sg], in_=g4[:, sg]))
                    p.wait("vector", "sSt")
                    p.wait("scalar", "sSt")
                p.wait("scalar", "sV")
                s(lambda sg=sg: S.activation(trh[:], g4[:, sg, 0:2048], AF.Square,
                                             accum_out=sth2[:, 2 + sg:3 + sg]))
                s(lambda sg=sg: S.activation(trh[:], g4[:, sg, 2048:4096], AF.Square,
                                             accum_out=sth2[:, 24 + sg:25 + sg]))
                p.wait("vector", "sS")
                vc(lambda sg=sg: V.tensor_add(sth2[:, 2 + sg:3 + sg], sth2[:, 2 + sg:3 + sg],
                                             sth2[:, 24 + sg:25 + sg]))

    p.join()
    stP2.close(); OPEN_STACKS.remove(stP2)

    _emit_streams(blk, p)


def _emit_streams(blk, p):
    def runner(fns):
        def _f(_eng):
            for fn in fns:
                fn()
        return _f
    blk.sync(runner(p.streams["sync"]))
    blk.vector(runner(p.streams["vector"]))
    blk.scalar(runner(p.streams["scalar"]))
    blk.tensor(runner(p.streams["tensor"]))
    blk.gpsimd(runner(p.streams["gpsimd"]))


def host_inputs(inputs):
    return {
        "hfT": np.ascontiguousarray(np.asarray(inputs["L0_attn_h_filter"]).T),
        "gfT": np.ascontiguousarray(np.asarray(inputs["L0_attn_g_filter"]).T),
        "hsT": np.ascontiguousarray(np.asarray(inputs["L0_attn_h_synth"]).T),
        "gsT": np.ascontiguousarray(np.asarray(inputs["L0_attn_g_synth"]).T),
        "gna": np.asarray(inputs["L0_attn_norm_gamma"]),
        "bna": np.asarray(inputs["L0_attn_norm_beta"]),
        "g1": np.asarray(inputs["L0_ffn_ln1_g"]),
        "c1": np.asarray(inputs["L0_ffn_ln1_b"]),
        "b1": np.asarray(inputs["L0_ffn_b1"]),
        "g2": np.asarray(inputs["L0_ffn_ln2_g"]),
        "c2": np.asarray(inputs["L0_ffn_ln2_b"]),
        "b2": np.asarray(inputs["L0_ffn_b2"]),
        "w1t": np.ascontiguousarray(np.asarray(inputs["L0_ffn_w1"]).T),
        "w2t": np.ascontiguousarray(np.asarray(inputs["L0_ffn_w2"]).T),
    }


def kernel(**inputs):
    x = np.ascontiguousarray(np.asarray(inputs["L0_attn_x"], dtype=np.float32))
    nc = build_program()
    base = host_inputs(inputs)
    in_maps = []
    for core in range(8):
        b, hf = core // 2, core % 2
        im = dict(base)
        im["x"] = np.ascontiguousarray(x[b])
        im["xo"] = np.ascontiguousarray(x[b, hf * OWN:(hf + 1) * OWN])
        in_maps.append(im)
    res = run_bass_kernel_spmd(nc, in_maps, core_ids=list(range(8)))
    out = np.empty((4, S_SEQ, D), dtype=np.float32)
    for core in range(8):
        b, hf = core // 2, core % 2
        out[b, hf * OWN:(hf + 1) * OWN] = res.results[core]["y"]
    return out
